# version 02
# baseline (speedup 1.0000x reference)
import contextlib
import numpy as np
import ml_dtypes
import concourse.bass as bass
import concourse.mybir as mybir
from concourse.bass_utils import run_bass_kernel_spmd

F32 = mybir.dt.float32
BF16 = mybir.dt.bfloat16
F16 = mybir.dt.float16
I32 = mybir.dt.int32
I8 = mybir.dt.int8
AF = mybir.ActivationFunctionType
ALU = mybir.AluOpType
f32 = np.float32
BF = ml_dtypes.bfloat16

D = 2048
S = 4096
NT = 1024
NH = 16
DH = 128
NKC = 16
IFFN = 5632
OGU = 2 * IFFN
C_EXP = 15.0
SCL = float(1.0 / np.sqrt(128.0))
LN_EPS = 1e-5
RMS_EPS = 1e-6
GU_SL = OGU // 8
DW_SL = D // 8

PI_H = np.array([8 * c + j for j in range(8) for c in range(256)], dtype=np.int64)
PI_I = np.array([8 * c + j for j in range(8) for c in range(704)], dtype=np.int64)


class KB:
    def __init__(self, nc, block, es):
        self.nc, self.block, self.es = nc, block, es
        self._h, self._c = {}, {}

    def sem(self, name):
        if name not in self._h:
            self._h[name] = self.es.enter_context(self.nc.semaphore(name))
            self._c[name] = 0
        return self._h[name]

    def dma(self, eng, out, in_, sname):
        h = self.sem(sname)
        eng.dma_start(out=out, in_=in_).then_inc(h, 16)
        self._c[sname] += 16
        return self._c[sname]

    def inc(self, inst, sname, n=1):
        inst.then_inc(self.sem(sname), n)
        self._c[sname] += n
        return self._c[sname]

    def c(self, sname):
        self.sem(sname)
        return self._c[sname]

    def wait(self, eng, sname, val=None):
        eng.wait_ge(self.sem(sname), self._c[sname] if val is None else val)



def emit_layer(kb, t, cfg):
    nc, block = kb.nc, kb.block
    L = cfg["tag"]
    kvc = cfg["kv_chunks"]
    pe = cfg["pe"]
    P = t["P"]

    x_resid, x_work = t["x_resid"], t["x_work"]
    qf = t["qf"]
    af = t["af"]
    wcache, wslab = t["wcache"], t["wslab"]

    def proj_fm(tag, w_ap, n_oc, evac, rhs=None, war=None, gate_mm=None,
                psum_war=None):
        if rhs is None:
            rhs = x_work
        s_w, s_cv, s_ev = tag + "_w", tag + "_cv", tag + "_ev"
        kb.sem(s_ev); kb.sem("slabcv")
        n_grp = (n_oc + 1) // 2
        for g in range(n_grp):
            oc0 = g * 2
            ocn = min(2, n_oc - oc0)
            wcols = ocn * 128
            cvm = []
            for kc in range(NKC):
                slabv = max(0, kb.c("slabcv") - 1)
                sw = "slab%d" % (kc % 2)
                def _w(sync, v=slabv, kc=kc, oc0=oc0, wcols=wcols, sw=sw):
                    kb.wait(sync, "slabcv", v)
                    kb.dma(sync, wslab[:, (kc % 2) * 512:(kc % 2) * 512 + wcols],
                           w_ap[kc * 128:(kc + 1) * 128, oc0 * 128:oc0 * 128 + wcols], sw)
                block.sync(_w)
                wv = kb.c(sw)
                def _cv(vector, kc=kc, wv=wv, wcols=wcols, g=g, sw=sw):
                    kb.wait(vector, sw, wv)
                    if kc == 0 and g > 0:
                        kb.wait(vector, tag + "_mm", g)
                    if kc == 0 and g == 0 and war is not None:
                        kb.wait(vector, war[0], war[1])
                    i = vector.tensor_copy(out=wcache[:, kc * wcols:(kc + 1) * wcols],
                                           in_=wslab[:, (kc % 2) * 512:(kc % 2) * 512 + wcols])
                    kb.inc(i, s_cv, 1)
                    kb.inc(vector.engine_nop(), "slabcv", 1)
                block.vector(_cv)
                cvm.append(kb.c(s_cv))
            ev_before = kb.c(s_ev)
            def _mm(tensor, g=g, ocn=ocn, wcols=wcols, cvm=cvm, ev_before=ev_before):
                if g > 0:
                    kb.wait(tensor, s_ev, ev_before)
                if g == 0 and gate_mm is not None:
                    kb.wait(tensor, gate_mm[0], gate_mm[1])
                if g == 0 and psum_war is not None:
                    kb.wait(tensor, psum_war[0], psum_war[1])
                last = None
                for kc in range(NKC):
                    kb.wait(tensor, s_cv, cvm[kc])
                    for oi in range(ocn):
                        for tg in range(2):
                            last = tensor.matmul(
                                P[oi][:, tg * 512:(tg + 1) * 512],
                                wcache[:, kc * wcols + oi * 128:kc * wcols + (oi + 1) * 128],
                                rhs[:, kc * NT + tg * 512:kc * NT + (tg + 1) * 512],
                                start=(kc == 0), stop=(kc == NKC - 1))
                kb.inc(last, tag + "_mm", 1)
            block.tensor(_mm)
            mmc = kb.c(tag + "_mm")
            for oi in range(ocn):
                evac(oc0 + oi, P[oi], mmc)

    if cfg.get("stage") == "dq":
        return
    def evac_q(oc, ps, mmc):
        def _(vector, oc=oc, ps=ps, mmc=mmc):
            kb.wait(vector, L + "q_mm", mmc)
            if pe:
                i = vector.scalar_tensor_tensor(
                    out=qf[:, oc * NT:(oc + 1) * NT], in0=ps[:, :],
                    scalar=t["bq"][:, oc:oc + 1], in1=t["pe_own"][:, :],
                    op0=ALU.add, op1=ALU.add)
            else:
                i = vector.tensor_scalar(
                    out=qf[:, oc * NT:(oc + 1) * NT], in0=ps[:, :],
                    scalar1=t["bq"][:, oc:oc + 1], scalar2=None, op0=ALU.add)
            kb.inc(i, L + "q_ev", 1)
            kb.inc(vector.engine_nop(), L + "q_ready", 1)
        block.vector(_)
    kb.sem(L + "q_ready")
    proj_fm(L + "q", cfg["wq"], NH, evac_q, gate_mm=cfg.get("entry_gate"))
    if cfg.get("stage") in ("q", "q_nodq"):
        return

    kst = t["kstage"]
    def evac_k(oc, ps, mmc):
        def _(vector, oc=oc, ps=ps, mmc=mmc):
            kb.wait(vector, L + "k_mm", mmc)
            if oc >= 1:
                kb.wait(vector, L + "k_st", oc * 16)
            sl = kst[:, 0:NT]
            if pe:
                i = vector.scalar_tensor_tensor(
                    out=sl, in0=ps[:, :], scalar=t["bk"][:, oc:oc + 1],
                    in1=t["pe_own"][:, :], op0=ALU.add, op1=ALU.add)
            else:
                i = vector.tensor_scalar(out=sl, in0=ps[:, :],
                                         scalar1=t["bk"][:, oc:oc + 1], scalar2=None,
                                         op0=ALU.add)
            kb.inc(i, L + "k_evd", 1)
            kb.inc(vector.engine_nop(), L + "k_ev", 1)
        block.vector(_)
        evd = kb.c(L + "k_evd")
        def _d(gpsimd, oc=oc, evd=evd):
            kb.wait(gpsimd, L + "k_evd", evd)
            kb.dma(nc.gpsimd, t["K_in"][oc * 128:(oc + 1) * 128, :],
                   kst[:, 0:NT], L + "k_st")
        block.gpsimd(_d)
    kb.sem(L + "k_st"); kb.sem(L + "k_evd")
    proj_fm(L + "k", cfg["wk"], kvc, evac_k, war=(L + "q_mm", kb.c(L + "q_mm")),
            psum_war=(L + "q_ev", kb.c(L + "q_ev")))
    if cfg.get("stage") == "qk":
        return

    VW = 256
    vcols = kvc * 128
    n_vg = vcols // VW
    vst = t["vstage"]
    kb.sem(L + "v_st"); kb.sem(L + "v_cv"); kb.sem(L + "v_ev"); kb.sem(L + "v_mm")
    for vg in range(n_vg):
        vw = VW
        cvm = []
        for kc in range(NKC):
            slabv = max(0, kb.c("slabcv") - 1)
            sw = "slab%d" % (kc % 2)
            def _w(sync, v=slabv, kc=kc, vg=vg, vw=vw, sw=sw):
                kb.wait(sync, "slabcv", v)
                kb.dma(sync, wslab[:, (kc % 2) * 512:(kc % 2) * 512 + vw],
                       cfg["wv"][kc * 128:(kc + 1) * 128, vg * VW:vg * VW + vw], sw)
            block.sync(_w)
            wv = kb.c(sw)
            def _cv(vector, kc=kc, wv=wv, vg=vg, vw=vw, sw=sw):
                kb.wait(vector, sw, wv)
                if kc == 0:
                    kb.wait(vector, L + "v_mm", kb.c(L + "v_mm"))
                    if vg == 0:
                        kb.wait(vector, L + "k_mm", kb.c(L + "k_mm"))
                i = vector.tensor_copy(out=wcache[:, kc * VW:kc * VW + vw],
                                       in_=wslab[:, (kc % 2) * 512:(kc % 2) * 512 + vw])
                kb.inc(i, L + "v_cv", 1)
                kb.inc(vector.engine_nop(), "slabcv", 1)
            block.vector(_cv)
            cvm.append(kb.c(L + "v_cv"))
        for tt in range(8):
            pid = tt % 4
            ev_before = kb.c(L + "v_ev")
            entc = kb.c("entry")
            kevc = kb.c(L + "k_ev")
            def _mm(tensor, tt=tt, pid=pid, vg=vg, vw=vw, cvm=cvm, ev_before=ev_before,
                    entc=entc, kevc=kevc):
                if tt == 0 and vg == 0:
                    kb.wait(tensor, "entry", entc)
                    kb.wait(tensor, L + "k_ev", kevc)
                if tt >= 4 or vg > 0:
                    kb.wait(tensor, L + "v_ev", max(0, ev_before - 3))
                for kc in range(NKC):
                    kb.wait(tensor, L + "v_cv", cvm[kc])
                    tensor.matmul(P[pid][:, 0:vw],
                                  x_work[:, kc * NT + tt * 128:kc * NT + (tt + 1) * 128],
                                  wcache[:, kc * VW:kc * VW + vw],
                                  start=(kc == 0), stop=False)
                i = tensor.matmul(P[pid][:, 0:vw], t["ones_1t"][0:1, :],
                                  t["bvrow"][0:1, vg * VW:vg * VW + vw],
                                  start=False, stop=True)
                kb.inc(i, L + "v_mm", 1)
            block.tensor(_mm)
            mmc = kb.c(L + "v_mm")
            def _ev(vector, tt=tt, pid=pid, vg=vg, vw=vw, mmc=mmc):
                kb.wait(vector, L + "v_mm", mmc)
                kb.wait(vector, L + "v_st", kb.c(L + "v_st"))
                i = vector.tensor_copy(out=vst[:, 0:vw], in_=P[pid][:, 0:vw])
                kb.inc(i, L + "v_evd", 1)
                kb.inc(vector.engine_nop(), L + "v_ev", 1)
            kb.sem(L + "v_evd")
            block.vector(_ev)
            evd = kb.c(L + "v_evd")
            def _d(gpsimd, tt=tt, vg=vg, vw=vw, evd=evd):
                kb.wait(gpsimd, L + "v_evd", evd)
                kb.dma(nc.gpsimd, t["V_in"][tt * 128:(tt + 1) * 128, vg * VW:vg * VW + vw],
                       vst[:, 0:vw], L + "v_st")
            block.gpsimd(_d)

    kstc, vstc = kb.c(L + "k_st"), kb.c(L + "v_st")
    def _cc(gpsimd):
        kb.wait(gpsimd, L + "k_st", kstc)
        kb.wait(gpsimd, L + "v_st", vstc)
        i = gpsimd.collective_compute(
            "AllGather", ALU.bypass, replica_groups=[[0, 1, 2, 3], [4, 5, 6, 7]],
            ins=[t["K_in"].opt()], outs=[t["K_g"].opt()])
        kb.inc(i, L + "ccK", 1)
        i = gpsimd.collective_compute(
            "AllGather", ALU.bypass, replica_groups=[[0, 1, 2, 3], [4, 5, 6, 7]],
            ins=[t["V_in"].opt()], outs=[t["V_g"].opt()])
        kb.inc(i, L + "ccV", 1)
    block.gpsimd(_cc)

    if cfg.get("stage") == "qkv":
        return
    kring, vring = t["kring"], t["vring"]
    RING = t["RING"]
    probs = t["probs"]
    s_kv = L + "a_kv"; s_cons = L + "a_cons"; s_exp = L + "a_exp"
    s_msk = L + "a_msk"; s_av = L + "a_av"; s_z = L + "a_z"; s_afe = L + "a_afe"
    for nm in (s_kv, s_cons, s_exp, s_msk, s_av, s_z, s_afe):
        kb.sem(nm)
    kblk = kvc * 128
    q_ready = kb.c(L + "q_ready")
    ccK, ccV = kb.c(L + "ccK"), kb.c(L + "ccV")
    vmm_all = kb.c(L + "v_mm")
    def _ldm(sync):
        kb.wait(sync, L + "v_mm", vmm_all)
        kb.dma(sync, t["masks"], t["masks_dram"][:, :], L + "a_mload")
    kb.sem(L + "a_mload")
    block.sync(_ldm)
    mload = kb.c(L + "a_mload")

    ld_i = 0
    head_order = list(range(NH))
    first_h = head_order[0]
    for hpos, h in enumerate(head_order):
        kh = h // (NH // kvc) if kvc < NH else h
        for kap in range(32):
            m = kap // 4
            c0 = 128 * m
            slot = ld_i % RING
            skv = L + "a_kv%d" % slot
            cons_need = max(0, (ld_i - RING + 1))
            def _ld(sync, kap=kap, kh=kh, slot=slot, cons_need=cons_need, skv=skv):
                if cons_need:
                    kb.wait(sync, s_av, cons_need)
                if kap == 0 and h == first_h:
                    kb.wait(sync, L + "ccK", ccK)
                    kb.wait(sync, L + "ccV", ccV)
                kb.dma(sync, kring[:, slot * 128:(slot + 1) * 128],
                       t["K_g"][kblk * (kap % 4) + 128 * kh: kblk * (kap % 4) + 128 * (kh + 1),
                                128 * (kap // 4):128 * (kap // 4 + 1)], skv)
                kb.dma(sync, vring[:, slot * 128:(slot + 1) * 128],
                       t["V_g"][1024 * (kap % 4) + 128 * (kap // 4):
                                1024 * (kap % 4) + 128 * (kap // 4) + 128,
                                128 * kh:128 * (kh + 1)], skv)
            block.sync(_ld)
            ldc = kb.c(skv)
            ld_i += 1

            stp = P[kap % 2]
            pieces = [(c0, 512), (512, 1024)] if c0 < 512 else [(c0, 1024)]
            expc_need = kb.c(s_exp)
            def _st(tensor, kap=kap, h=h, slot=slot, stp=stp, pieces=pieces,
                    ldc=ldc, expc_need=expc_need, skv=skv):
                kb.wait(tensor, skv, ldc)
                if h == first_h and kap == 0:
                    kb.wait(tensor, L + "q_ready", q_ready)
                    kb.wait(tensor, L + "v_ev", kb.c(L + "v_ev"))
                if kap >= 2 or hpos > 0:
                    kb.wait(tensor, s_exp, max(0, expc_need - 1))
                last = None
                for (a, b) in pieces:
                    last = tensor.matmul(stp[:, a:b],
                                         kring[:, slot * 128:(slot + 1) * 128],
                                         qf[:, h * NT + a:h * NT + b],
                                         start=True, stop=True)
                kb.inc(last, L + "a_st", 1)
            kb.sem(L + "a_st")
            block.tensor(_st)
            stc = kb.c(L + "a_st")
            def _ex(scalar, kap=kap, stp=stp, c0=c0):
                kb.wait(scalar, L + "a_st", stc)
                if kap >= 2 or hpos > 0:
                    kb.wait(scalar, s_av, max(0, kb.c(s_av) - 1))
                i = scalar.activation(
                    out=probs[:, (kap % 2) * NT + c0:(kap % 2) * NT + NT],
                    in_=stp[:, c0:NT], func=AF.Exp, scale=f32(SCL), bias=f32(-C_EXP))
                kb.inc(i, s_exp, 1)
            block.scalar(_ex)
            expc = kb.c(s_exp)
            rho = kap % 4
            def _mk(vector, kap=kap, c0=c0, m=m, rho=rho):
                kb.wait(vector, s_exp, expc)
                if h == first_h and kap == 0:
                    kb.wait(vector, L + "a_mload", mload)
                i = vector.tensor_tensor(
                    out=probs[:, (kap % 2) * NT + c0:(kap % 2) * NT + c0 + 128],
                    in0=probs[:, (kap % 2) * NT + c0:(kap % 2) * NT + c0 + 128],
                    in1=t["masks"][:, (m * 4 + rho) * 128:(m * 4 + rho + 1) * 128],
                    op=ALU.mult)
                kb.inc(i, s_msk, 1)
            block.vector(_mk)
            mskc = kb.c(s_msk)
            def _av(tensor, kap=kap, slot=slot, pieces=pieces, mskc=mskc, h=h):
                kb.wait(tensor, s_msk, mskc)
                if kap == 0 and h == first_h:
                    kb.wait(tensor, "entry", 1)
                if kap == 0:
                    kb.wait(tensor, s_afe, kb.c(s_afe))
                for (a, b) in pieces:
                    tensor.matmul(P[3][0:1, a:b], t["ones_k"][:, 0:1],
                                  probs[:, (kap % 2) * NT + a:(kap % 2) * NT + b],
                                  start=(kap == 0), stop=(kap == 31),
                                  skip_group_check=True)
                last = None
                for (a, b) in pieces:
                    last = tensor.matmul(P[2][:, a:b],
                                         vring[:, slot * 128:(slot + 1) * 128],
                                         probs[:, (kap % 2) * NT + a:(kap % 2) * NT + b],
                                         start=(kap == 0), stop=(kap == 31),
                                         skip_group_check=True)
                kb.inc(last, s_av, 1)
            block.tensor(_av)
        avc = kb.c(s_av)
        def _zr(vector, h=h, avc=avc):
            kb.wait(vector, s_av, avc)
            i = vector.reciprocal(out=t["rows"][0:1, 0:1024], in_=P[3][0:1, 0:1024])
            kb.inc(i, s_z, 1)
        block.vector(_zr)
        zc = kb.c(s_z)
        def _bc(tensor, h=h, zc=zc):
            kb.wait(tensor, s_z, zc)
            last = None
            for tg in range(2):
                last = tensor.matmul(P[3][:, tg * 512:(tg + 1) * 512],
                                     t["ones_1t_f"][0:1, :],
                                     t["rows"][0:1, tg * 512:(tg + 1) * 512],
                                     start=True, stop=True)
            kb.inc(last, L + "a_bc", 1)
        kb.sem(L + "a_bc")
        block.tensor(_bc)
        bcc = kb.c(L + "a_bc")
        def _ae(vector, h=h, bcc=bcc):
            kb.wait(vector, L + "a_bc", bcc)
            if h == first_h:
                kb.wait(vector, L + "dq_done", kb.c(L + "dq_done"))
            vector.tensor_copy(out=t["bc_sb"][:, 0:1024], in_=P[3][:, 0:1024])
            i = vector.tensor_tensor(out=af[:, h * NT:(h + 1) * NT],
                                     in0=P[2][:, 0:1024], in1=t["bc_sb"][:, 0:1024],
                                     op=ALU.mult)
            kb.inc(i, s_afe, 1)
        block.vector(_ae)

    if cfg.get("stage") in ("attn", "attn_nodq"):
        return
    afe_all = kb.c(s_afe)
    def evac_wo(oc, ps, mmc):
        def _(vector, oc=oc, ps=ps, mmc=mmc):
            kb.wait(vector, L + "o_mm", mmc)
            if cfg["bo"] is not None:
                i = vector.scalar_tensor_tensor(
                    out=x_resid[:, oc * NT:(oc + 1) * NT], in0=ps[:, :],
                    scalar=t["bo"][:, oc:oc + 1], in1=x_resid[:, oc * NT:(oc + 1) * NT],
                    op0=ALU.add, op1=ALU.add)
            else:
                i = vector.tensor_tensor(
                    out=x_resid[:, oc * NT:(oc + 1) * NT], in0=ps[:, :],
                    in1=x_resid[:, oc * NT:(oc + 1) * NT], op=ALU.add)
            kb.inc(i, L + "o_ev", 1)
            kb.inc(vector.engine_nop(), L + "o_res", 1)
        block.vector(_)
    kb.sem(L + "o_res")
    def _gate(tensor):
        kb.wait(tensor, s_afe, afe_all)
    block.tensor(_gate)
    proj_fm(L + "o", cfg["wo"], NKC, evac_wo, rhs=af, war=(L + "v_mm", kb.c(L + "v_mm")),
            psum_war=(L + "a_afe", kb.c(L + "a_afe")))

    emit_ln(kb, t, L + "ln1", src=x_resid, out_bf=x_work, cols=NT,
            gate=(L + "o_res", kb.c(L + "o_res")))

    if cfg.get("stage") == "wo":
        return
    emit_ffn(kb, t, cfg)
    if cfg.get("stage") in ("ffnh", "ffny", "ffny1", "ffnh1", "rms0"):
        return


def emit_ln(kb, t, tag, src, out_bf, cols, col0=0, gate=None, store=None):
    nc, block = kb.nc, kb.block
    P = t["P"]
    sb = t["statbuf"]
    rows = t["rows"]
    s_sq = tag + "_sq"; s_mm = tag + "_mm"; s_row = tag + "_row"; s_bc = tag + "_bc"
    s_ap = tag + "_ap"
    for nm in (s_sq, s_mm, s_row, s_bc, s_ap):
        kb.sem(nm)
    npc = (cols + 511) // 512
    for kc in range(NKC):
        def _cast(vector, kc=kc):
            if kc == 0 and gate is not None:
                kb.wait(vector, gate[0], gate[1])
            if kc >= 1:
                kb.wait(vector, s_mm, kb.c(s_mm))
            vector.tensor_copy(out=sb[:, 0:cols],
                               in_=src[:, kc * NT + col0:kc * NT + col0 + cols])
            kb.inc(vector.engine_nop(), s_sq, 1)
        block.vector(_cast)
        def _sqr(scalar, kc=kc):
            if kc == 0 and gate is not None:
                kb.wait(scalar, gate[0], gate[1])
            if kc >= 1:
                kb.wait(scalar, s_mm, kb.c(s_mm))
            i = scalar.activation(out=sb[:, 1024:1024 + cols],
                                  in_=src[:, kc * NT + col0:kc * NT + col0 + cols],
                                  func=AF.Square)
            kb.inc(i, s_sq, 1)
        block.scalar(_sqr)
        sqc = kb.c(s_sq)
        def _mm(tensor, kc=kc, sqc=sqc):
            if kc == 0 and gate is not None:
                kb.wait(tensor, gate[0], gate[1])
            if kc == 0:
                kb.wait(tensor, "entry", 1)
            kb.wait(tensor, s_sq, sqc)
            last = None
            for pc in range(npc):
                a, b = pc * 512, min(cols, (pc + 1) * 512)
                tensor.matmul(P[0][0:1, a:b], t["ones_k"][:, 0:1],
                              sb[:, a:b], start=(kc == 0), stop=(kc == NKC - 1))
                last = tensor.matmul(P[1][0:1, a:b], t["ones_k"][:, 0:1],
                                     sb[:, 1024 + a:1024 + b],
                                     start=(kc == 0), stop=(kc == NKC - 1))
            kb.inc(last, s_mm, 1)
        block.tensor(_mm)
    mmc = kb.c(s_mm)
    def _row(vector, mmc=mmc):
        kb.wait(vector, s_mm, mmc)
        vector.tensor_scalar(out=rows[0:1, 0:cols], in0=P[0][0:1, 0:cols],
                             scalar1=1.0 / 2048.0, scalar2=None, op0=ALU.mult)
        vector.tensor_scalar(out=rows[0:1, 1024:1024 + cols], in0=P[1][0:1, 0:cols],
                             scalar1=1.0 / 2048.0, scalar2=None, op0=ALU.mult)
        vector.tensor_tensor(out=rows[0:1, 2048:2048 + cols], in0=rows[0:1, 0:cols],
                             in1=rows[0:1, 0:cols], op=ALU.mult)
        i = vector.tensor_tensor(out=rows[0:1, 1024:1024 + cols],
                                 in0=rows[0:1, 1024:1024 + cols],
                                 in1=rows[0:1, 2048:2048 + cols], op=ALU.subtract)
        kb.inc(i, s_row, 1)
    block.vector(_row)
    rc = kb.c(s_row)
    def _rstd(scalar, rc=rc):
        kb.wait(scalar, s_row, rc)
        scalar.activation(out=rows[0:1, 2048:2048 + cols],
                          in_=rows[0:1, 1024:1024 + cols], func=AF.Ln,
                          bias=f32(LN_EPS))
        i = scalar.activation(out=rows[0:1, 1024:1024 + cols],
                              in_=rows[0:1, 2048:2048 + cols], func=AF.Exp,
                              scale=f32(-0.5))
        kb.inc(i, s_row, 1)
    block.scalar(_rstd)
    rc2 = kb.c(s_row)
    def _fin(vector, rc2=rc2):
        kb.wait(vector, s_row, rc2)
        i = vector.tensor_tensor(out=rows[0:1, 2048:2048 + cols],
                                 in0=rows[0:1, 0:cols],
                                 in1=rows[0:1, 1024:1024 + cols], op=ALU.mult)
        kb.inc(i, s_row, 1)
    block.vector(_fin)
    rc3 = kb.c(s_row)
    def _bc(tensor, rc3=rc3):
        kb.wait(tensor, s_row, rc3)
        last = None
        for pc in range(npc):
            a, b = pc * 512, min(cols, (pc + 1) * 512)
            tensor.matmul(P[0][:, a:b], t["ones_1t_f"][0:1, :],
                          rows[0:1, 1024 + a:1024 + b], start=True, stop=True)
            last = tensor.matmul(P[1][:, a:b], t["ones_1t_f"][0:1, :],
                                 rows[0:1, 2048 + a:2048 + b], start=True, stop=True)
        kb.inc(last, s_bc, 1)
    block.tensor(_bc)
    bcc = kb.c(s_bc)
    def _bcev(vector, bcc=bcc):
        kb.wait(vector, s_bc, bcc)
        vector.tensor_copy(out=t["bc_sb"][:, 0:cols], in_=P[0][:, 0:cols])
        i = vector.tensor_copy(out=t["bc_sb"][:, 1024:1024 + cols], in_=P[1][:, 0:cols])
        kb.inc(i, s_bc, 1)
    block.vector(_bcev)
    bcc2 = kb.c(s_bc)
    for kc in range(NKC):
        def _ap(vector, kc=kc):
            kb.wait(vector, s_bc, bcc2)
            vector.tensor_tensor(out=t["tmpf"][:, 0:cols],
                                 in0=src[:, kc * NT + col0:kc * NT + col0 + cols],
                                 in1=t["bc_sb"][:, 0:cols], op=ALU.mult)
            if store is None:
                dst = src[:, kc * NT + col0:kc * NT + col0 + cols]
            else:
                kb.wait(vector, tag + "_st", kb.c(tag + "_st"))
                dst = t["tmpf2"][:, 0:cols]
            i = vector.tensor_tensor(out=dst,
                                     in0=t["tmpf"][:, 0:cols],
                                     in1=t["bc_sb"][:, 1024:1024 + cols],
                                     op=ALU.subtract)
            kb.inc(i, tag + "_apv", 1)
        kb.sem(tag + "_st"); kb.sem(tag + "_apv")
        block.vector(_ap)
        apvn = kb.c(tag + "_apv")
        if store is not None:
            def _std(gpsimd, kc=kc, apvn=apvn):
                kb.wait(gpsimd, tag + "_apv", apvn)
                kb.dma(gpsimd, store[kc * 128:(kc + 1) * 128, col0:col0 + cols],
                       t["tmpf2"][:, 0:cols], tag + "_st")
            block.gpsimd(_std)
            def _inc(scalar, apvn=apvn):
                kb.wait(scalar, tag + "_apv", apvn)
                kb.inc(scalar.nop(), s_ap, 1)
            block.scalar(_inc)
        else:
            def _cast(scalar, kc=kc, apvn=apvn):
                kb.wait(scalar, tag + "_apv", apvn)
                i = scalar.activation(out=out_bf[:, kc * NT + col0:kc * NT + col0 + cols],
                                      in_=src[:, kc * NT + col0:kc * NT + col0 + cols],
                                      func=AF.Copy, bias=0.0)
                kb.inc(i, s_ap, 1)
            block.scalar(_cast)


def emit_ffn(kb, t, cfg):
    nc, block = kb.nc, kb.block
    L = cfg["tag"]
    P = t["P"]
    x_resid, x_work = t["x_resid"], t["x_work"]
    h_sb = t["h_sb"]
    y_sb = t["y_sb"]
    gu_w, dw_w = t[L + "gu_w"], t[L + "dw_w"]
    ws = t["wstream"]
    s_w = L + "f_w"; s_mm = L + "f_mm"; s_sg = L + "f_sg"; s_h = L + "f_h"
    s_ev = L + "f_ev"; s_y = L + "f_y"; s_mmsub = L + "f_mmsub"
    s_set = L + "f_set"
    for nm in (s_w, s_mm, s_sg, s_h, s_ev, s_y, s_mmsub, s_set):
        kb.sem(nm)
    ccGU, ccDW = kb.c(L + "ccGU"), kb.c(L + "ccDW")

    for tg in range(2):
        tcol = tg * 512
        order = []
        for ic in range(44):
            order += [("g", ic), ("u", ic)]
        omm_all = kb.c(L + "o_mm")
        for idx, (gu, ic) in enumerate(order):
            oc = ic if gu == "g" else ic + 44
            rb, cl = oc // 11, (oc % 11) * 128
            slot = idx % 3
            pid = idx % 4
            sws = L + "f_w%d" % slot
            mmsub_now = kb.c(s_mmsub)
            def _ld(sync, oc=oc, rb=rb, cl=cl, slot=slot, idx=idx, tg=tg, sws=sws,
                    mmsub_now=mmsub_now):
                if idx == 0 and tg == 0:
                    kb.wait(sync, L + "ccGU", ccGU)
                    kb.wait(sync, L + "o_mm", omm_all)
                if idx < 3 and tg == 1:
                    kb.wait(sync, s_mmsub, mmsub_now)
                if idx >= 3:
                    kb.wait(sync, s_mm, max(0, kb.c(s_mm) - 2))
                for kq in range(0, NKC, 4):
                    kb.dma(sync, ws[:, slot * 2048 + kq * 128:slot * 2048 + (kq + 4) * 128],
                           gu_w[rb * 2048 + kq * 128:rb * 2048 + (kq + 4) * 128, cl:cl + 128]
                           .rearrange("(kc p) o -> p kc o", p=128), sws)
            block.sync(_ld)
            ldc = kb.c(sws)
            ln1ap = kb.c(L + "ln1_ap")
            def _mm(tensor, pid=pid, slot=slot, ldc=ldc, idx=idx, tg=tg, ln1ap=ln1ap,
                    sws=sws):
                kb.wait(tensor, sws, ldc)
                if idx == 0 and tg == 0:
                    kb.wait(tensor, L + "ln1_ap", ln1ap)
                if idx == 0 and tg == 1:
                    kb.wait(tensor, L + "ln2tg0_ap", kb.c(L + "ln2tg0_ap"))
                if idx >= 4 or tg > 0:
                    kb.wait(tensor, s_ev, max(0, kb.c(s_ev) - 3))
                pass
                last = None
                for kc in range(NKC):
                    last = tensor.matmul(P[pid][:, 0:512],
                                         ws[:, slot * 2048 + kc * 128:slot * 2048 + (kc + 1) * 128],
                                         x_work[:, kc * NT + tcol:kc * NT + tcol + 512],
                                         start=(kc == 0), stop=(kc == NKC - 1))
                kb.inc(last, s_mm, 1)
            block.tensor(_mm)
            mmc = kb.c(s_mm)
            if gu == "g":
                def _sg(scalar, pid=pid, mmc=mmc, ic=ic):
                    kb.wait(scalar, s_mm, mmc)
                    if ic > 0:
                        kb.wait(scalar, s_h, kb.c(s_h))
                    i = scalar.activation(out=t["sg_sb"][:, 0:512], in_=P[pid][:, 0:512],
                                          func=AF.Silu)
                    kb.inc(i, s_sg, 1)
                    kb.inc(scalar.nop(), s_ev, 1)
                block.scalar(_sg)
            else:
                sgc = kb.c(s_sg)
                def _h(vector, pid=pid, mmc=mmc, ic=ic, sgc=sgc):
                    kb.wait(vector, s_mm, mmc)
                    kb.wait(vector, s_sg, sgc)
                    i = vector.tensor_tensor(out=h_sb[:, ic * 512:(ic + 1) * 512],
                                             in0=t["sg_sb"][:, 0:512], in1=P[pid][:, 0:512],
                                             op=ALU.mult)
                    kb.inc(i, s_h, 1)
                    kb.inc(vector.engine_nop(), s_ev, 1)
                block.vector(_h)
        hc_all = kb.c(s_h)
        if cfg.get("stage") == "ffnh" and tg == 0:
            return
        if cfg.get("stage") == "ffnh1" and tg == 1:
            return
        IC_SUB = [(0, 16), (16, 32), (32, 44)]
        sub_i = 0
        for hc in range(NKC):
            pid = hc % 4
            ldcs = []
            for (i0, i1) in IC_SUB:
                slot = sub_i % 3
                sws = L + "f_w%d" % slot
                gu_mm_now = kb.c(s_mm)
                first3 = sub_i % 48 < 3
                def _ld(sync, hc=hc, tg=tg, i0=i0, i1=i1, slot=slot, sub_i=sub_i,
                        sws=sws, gu_mm_now=gu_mm_now, first3=first3):
                    if hc == 0 and tg == 0 and i0 == 0:
                        kb.wait(sync, L + "ccDW", ccDW)
                    if first3:
                        kb.wait(sync, s_mm, gu_mm_now)
                    else:
                        kb.wait(sync, s_mmsub, max(0, kb.c(s_mmsub) - 2))
                    for q0 in range(i0, i1, 4):
                        q1 = min(i1, q0 + 4)
                        kb.dma(sync, ws[:, slot * 2048 + (q0 - i0) * 128:slot * 2048 + (q1 - i0) * 128],
                               dw_w[(hc // 2) * 5632 + q0 * 128:(hc // 2) * 5632 + q1 * 128,
                                    (hc % 2) * 128:(hc % 2) * 128 + 128]
                               .rearrange("(ic p) o -> p ic o", p=128), sws)
                block.sync(_ld)
                ldc_now = kb.c(sws)
                def _settle(gpsimd, sws=sws, ldc_now=ldc_now):
                    kb.wait(gpsimd, sws, ldc_now)
                    kb.inc(gpsimd.nop(), s_set, 1)
                block.gpsimd(_settle)
                ldcs.append((ldc_now, slot, i0, i1, kb.c(s_set)))
                sub_i += 1
            def _mm(tensor, hc=hc, pid=pid, ldcs=ldcs, tg=tg):
                if hc == 0:
                    kb.wait(tensor, s_h, hc_all)
                if hc >= 4 or tg > 0:
                    kb.wait(tensor, s_y, max(0, kb.c(s_y) - 3))
                for (ldc, slot, i0, i1, setc) in ldcs:
                    kb.wait(tensor, L + "f_w%d" % slot, ldc)
                    kb.wait(tensor, s_set, setc)
                    last = None
                    for ic in range(i0, i1):
                        last = tensor.matmul(P[pid][:, 512:1024],
                                             ws[:, slot * 2048 + (ic - i0) * 128:slot * 2048 + (ic - i0 + 1) * 128],
                                             h_sb[:, ic * 512:(ic + 1) * 512],
                                             start=(ic == 0), stop=(ic == 43))
                    kb.inc(last, s_mmsub, 1)
            block.tensor(_mm)
            mmc = kb.c(s_mmsub)
            def _ev(vector, hc=hc, pid=pid, mmc=mmc):
                kb.wait(vector, s_mmsub, mmc)
                i = vector.tensor_copy(out=y_sb[:, hc * 512:(hc + 1) * 512],
                                       in_=P[pid][:, 512:1024])
                kb.inc(i, s_y, 1)
            block.vector(_ev)
        if cfg.get("stage") == "ffny" and tg == 0:
            return
        if cfg.get("stage") == "ffny1" and tg == 1:
            return
        if cfg.get("stage") == "ffnh1" and tg == 1:
            pass
        emit_rms_resid(kb, t, cfg, tg)
        if cfg.get("stage") == "rms0" and tg == 0:
            return
        emit_ln(kb, t, L + "ln2tg%d" % tg, src=x_resid, out_bf=x_work, cols=512,
                col0=tcol, gate=(L + "rms%d_ap" % tg, kb.c(L + "rms%d_ap" % tg)),
                store=cfg.get("store"))


def emit_rms_resid(kb, t, cfg, tg):
    nc, block = kb.nc, kb.block
    L = cfg["tag"]
    P = t["P"]
    tcol = tg * 512
    y_sb = t["y_sb"]
    rows = t["rows"]
    tag = L + "rms%d" % tg
    s_sq = tag + "_sq"; s_mm = tag + "_mm"; s_row = tag + "_row"
    s_bc = tag + "_bc"; s_ap = tag + "_ap"
    for nm in (s_sq, s_mm, s_row, s_bc, s_ap):
        kb.sem(nm)
    yc = kb.c(L + "f_y")
    sb = t["statbuf"]
    for kc in range(NKC):
        def _sq(scalar, kc=kc):
            if kc == 0:
                kb.wait(scalar, L + "f_y", yc)
            if kc >= 1:
                kb.wait(scalar, s_mm, kb.c(s_mm))
            i = scalar.activation(out=sb[:, 0:512],
                                  in_=y_sb[:, kc * 512:(kc + 1) * 512], func=AF.Square)
            kb.inc(i, s_sq, 1)
        block.scalar(_sq)
        sqc = kb.c(s_sq)
        def _mm(tensor, kc=kc, sqc=sqc):
            if kc == 0:
                kb.wait(tensor, L + "f_y", yc)
            kb.wait(tensor, s_sq, sqc)
            i = tensor.matmul(P[0][0:1, 0:512], t["ones_k"][:, 0:1], sb[:, 0:512],
                              start=(kc == 0), stop=(kc == NKC - 1))
            kb.inc(i, s_mm, 1)
        block.tensor(_mm)
    mmc = kb.c(s_mm)
    def _row(vector, mmc=mmc):
        kb.wait(vector, s_mm, mmc)
        i = vector.tensor_scalar(out=rows[0:1, 0:512], in0=P[0][0:1, 0:512],
                                 scalar1=1.0 / 2048.0, scalar2=None, op0=ALU.mult)
        kb.inc(i, s_row, 1)
    block.vector(_row)
    rc = kb.c(s_row)
    def _rstd(scalar, rc=rc):
        kb.wait(scalar, s_row, rc)
        scalar.activation(out=rows[0:1, 2048:2560], in_=rows[0:1, 0:512],
                          func=AF.Ln, bias=f32(RMS_EPS))
        i = scalar.activation(out=rows[0:1, 0:512], in_=rows[0:1, 2048:2560],
                              func=AF.Exp, scale=f32(-0.5))
        kb.inc(i, s_row, 1)
    block.scalar(_rstd)
    rc3 = kb.c(s_row)
    def _bc(tensor, rc3=rc3):
        kb.wait(tensor, s_row, rc3)
        i = tensor.matmul(P[0][:, 0:512], t["ones_1t_f"][0:1, :], rows[0:1, 0:512],
                          start=True, stop=True)
        kb.inc(i, s_bc, 1)
    block.tensor(_bc)
    bcc = kb.c(s_bc)
    def _bcev(vector, bcc=bcc):
        kb.wait(vector, s_bc, bcc)
        i = vector.tensor_copy(out=t["bc_sb"][:, 0:512], in_=P[0][:, 0:512])
        kb.inc(i, s_bc, 1)
    block.vector(_bcev)
    bcc2 = kb.c(s_bc)
    for kc in range(NKC):
        def _ap(vector, kc=kc):
            kb.wait(vector, s_bc, bcc2)
            vector.tensor_tensor(out=t["tmpf"][:, 0:512],
                                 in0=y_sb[:, kc * 512:(kc + 1) * 512],
                                 in1=t["bc_sb"][:, 0:512], op=ALU.mult)
            i = vector.tensor_tensor(
                out=t["x_resid"][:, kc * NT + tcol:kc * NT + tcol + 512],
                in0=t["x_resid"][:, kc * NT + tcol:kc * NT + tcol + 512],
                in1=t["tmpf"][:, 0:512], op=ALU.add)
            kb.inc(i, s_ap, 1)
        block.vector(_ap)



def emit_dequant(kb, t, cfg):
    nc, block = kb.nc, kb.block
    L = cfg["tag"]
    dq = t["dq"]
    s_src = L + "dq_src"; s_u = L + "dq_u"; s_g = L + "dq_g"; s_err = L + "dq_err"
    s_w = L + "dq_w"; s_out = L + "dq_out"; s_done = L + "dq_done"
    for nm in (s_src, s_u, s_g, s_err, s_w, s_out, s_done):
        kb.sem(nm)

    it_box = [0]
    def one_matrix(src_ap, sc_ap, out_ap, n_ch, ncols, HW, shift_plan,
                   row_plan):
        nh = ncols // HW
        for q in range(n_ch):
            for hh in range(nh):
                it = it_box[0]
                slot = it % 2
                c0, c1 = hh * HW, (hh + 1) * HW
                uc_need = kb.c(s_u)
                def _ld(sync, q=q, hh=hh, slot=slot, it=it, uc_need=uc_need,
                        c0=c0, c1=c1):
                    if it >= 2:
                        kb.wait(sync, s_u, uc_need - 1)
                        kb.wait(sync, s_w, max(0, kb.c(s_w) - 1))
                    for (pr0, prn, sr0) in row_plan(q):
                        kb.dma(sync, dq["psrc"][pr0:pr0 + prn,
                                                slot * HW:slot * HW + HW],
                               src_ap[sr0:sr0 + prn, c0:c1], s_src + str(slot))
                    kb.dma(sync, dq["sc"][:, slot * HW:slot * HW + HW],
                           sc_ap[q * 128:(q + 1) * 128, c0:c1], s_src + str(slot))
                block.sync(_ld)
                ldc = kb.c(s_src + str(slot))
                def _int(vector, q=q, slot=slot, it=it, ldc=ldc):
                    kb.wait(vector, s_src + str(slot), ldc)
                    kb.wait(vector, s_err, kb.c(s_err))
                    if it >= 2:
                        kb.wait(vector, s_out + str(slot), kb.c(s_out + str(slot)))
                    ps = dq["psrc"][:, slot * HW:slot * HW + HW]
                    B, C = dq["B"][:, 0:HW], dq["C"][:, 0:HW]
                    ja, jb, mcol = shift_plan(q)
                    if jb is None:
                        vector.tensor_scalar(out=ps, in0=ps, scalar1=4 * ja,
                                             scalar2=15,
                                             op0=ALU.logical_shift_right,
                                             op1=ALU.bitwise_and)
                    else:
                        vector.tensor_scalar(out=B, in0=ps, scalar1=4 * ja, scalar2=15,
                                             op0=ALU.logical_shift_right,
                                             op1=ALU.bitwise_and)
                        vector.tensor_scalar(out=C, in0=ps, scalar1=4 * jb, scalar2=15,
                                             op0=ALU.logical_shift_right,
                                             op1=ALU.bitwise_and)
                        vector.tensor_scalar(out=B, in0=B, scalar1=mcol[0],
                                             scalar2=None, op0=ALU.bitwise_and)
                        vector.tensor_scalar(out=C, in0=C, scalar1=mcol[1],
                                             scalar2=None, op0=ALU.bitwise_and)
                        vector.tensor_tensor(out=ps, in0=B, in1=C,
                                             op=ALU.bitwise_or)
                    vector.tensor_scalar(out=B, in0=ps, scalar1=22,
                                         scalar2=0x01C00000,
                                         op0=ALU.logical_shift_left,
                                         op1=ALU.bitwise_and)
                    i = vector.tensor_scalar(out=C, in0=ps, scalar1=8, scalar2=28,
                                             op0=ALU.bitwise_and,
                                             op1=ALU.logical_shift_left)
                    kb.inc(i, s_u, 1)
                    i = vector.tensor_scalar(out=B, in0=B,
                                             scalar1=0x3F000000, scalar2=None,
                                             op0=ALU.add)
                    kb.inc(i, s_g, 1)
                block.vector(_int)
                gc = kb.c(s_g)
                def _err(scalar, gc=gc):
                    kb.wait(scalar, s_g, gc)
                    i = scalar.activation(out=dq["Derr"][:, 0:HW], in_=dq["Bf"][:, 0:HW],
                                          func=AF.Relu, scale=f32(-1.0), bias=f32(1.0))
                    kb.inc(i, s_err, 1)
                block.scalar(_err)
                ec = kb.c(s_err)
                def _fin(vector, slot=slot, ec=ec):
                    kb.wait(vector, s_err, ec)
                    vector.tensor_tensor(out=dq["Bf"][:, 0:HW], in0=dq["Bf"][:, 0:HW],
                                         in1=dq["Derr"][:, 0:HW], op=ALU.subtract)
                    vector.tensor_tensor(out=dq["B"][:, 0:HW], in0=dq["B"][:, 0:HW],
                                         in1=dq["C"][:, 0:HW], op=ALU.bitwise_or)
                    i = vector.tensor_tensor(out=dq["wt"][:, slot * HW:slot * HW + HW],
                                             in0=dq["Bf"][:, 0:HW],
                                             in1=dq["sc"][:, slot * HW:slot * HW + HW],
                                             op=ALU.mult)
                    kb.inc(i, s_w, 1)
                block.vector(_fin)
                wc = kb.c(s_w)
                def _st(gpsimd, q=q, slot=slot, c0=c0, c1=c1, wc=wc):
                    kb.wait(gpsimd, s_w, wc)
                    kb.dma(gpsimd, out_ap[q * 128:(q + 1) * 128, c0:c1],
                           dq["wt"][:, slot * HW:slot * HW + HW], s_out + str(slot))
                block.gpsimd(_st)
                it_box[0] = it + 1

    def gu_rows(q):
        return [(0, 128, (q % 2) * 128)]
    one_matrix(t[L + "gu_pk"], t[L + "gu_sc"], t[L + "gu_in"], 16, GU_SL, 704,
               lambda q: (q // 2, None, None), gu_rows)
    def dw_rows(q):
        base = (128 * q) % 704
        if base + 128 <= 704:
            return [(0, 128, base)]
        n1 = 704 - base
        return [(0, n1, base), (n1, 128 - n1, 0)]
    def dw_shift_plan(q):
        ja = (128 * q) // 704
        jb = (128 * q + 127) // 704
        if ja == jb:
            return (ja, None, None)
        return (ja, jb, (t["jmaskA"][:, q:q + 1], t["jmaskB"][:, q:q + 1]))
    one_matrix(t[L + "dw_pk"], t[L + "dw_sc"], t[L + "dw_in"], 44, DW_SL, 256,
               dw_shift_plan, dw_rows)

    outc0, outc1 = kb.c(s_out + "0"), kb.c(s_out + "1")
    def _cc(gpsimd, outc0=outc0, outc1=outc1):
        kb.wait(gpsimd, s_out + "0", outc0)
        kb.wait(gpsimd, s_out + "1", outc1)
        i = gpsimd.collective_compute(
            "AllGather", ALU.bypass, replica_groups=[list(range(8))],
            ins=[t[L + "gu_in"].opt()], outs=[t[L + "gu_w"].opt()])
        kb.inc(i, L + "ccGU", 1)
        i = gpsimd.collective_compute(
            "AllGather", ALU.bypass, replica_groups=[list(range(8))],
            ins=[t[L + "dw_in"].opt()], outs=[t[L + "dw_w"].opt()])
        kb.inc(i, L + "ccDW", 1)
        kb.inc(gpsimd.nop(), s_done, 1)
    block.gpsimd(_cc)


def build_launch(layers, final, entry_has_xw, stage=None, race_detect=False):
    es = contextlib.ExitStack()
    nc = bass.Bass("TRN2", target_bir_lowering=False,
                   detect_race_conditions=race_detect)
    t = {}
    names = []

    def inp(name, shape, dt):
        names.append(name)
        return nc.dram_tensor(name, shape, dt, kind="ExternalInput").ap()

    xin = inp("xin", [D, NT], F32)
    t["masks_in"] = None
    if entry_has_xw:
        xwin = inp("xwin", [D, NT], BF16)
    masks_in = inp("masks", [128, 32 * 128], BF16)
    t_masks_in = masks_in
    if stage == "dq":
        dbg_gu = nc.dram_tensor("dbg_gu", [8 * D, GU_SL], BF16, kind="ExternalOutput").ap()
        dbg_dw = nc.dram_tensor("dbg_dw", [8 * IFFN, DW_SL], BF16, kind="ExternalOutput").ap()
    elif stage == "ffnh":
        dbg_h = nc.dram_tensor("dbg_h", [IFFN, 512], BF16, kind="ExternalOutput").ap()
    elif stage in ("ffny", "ffny1"):
        dbg_y = nc.dram_tensor("dbg_y", [D, 512], BF16, kind="ExternalOutput").ap()
        dbg_dw = nc.dram_tensor("dbg_dw", [8 * IFFN, DW_SL], BF16, kind="ExternalOutput").ap()
        dbg_h = nc.dram_tensor("dbg_h", [IFFN, 512], BF16, kind="ExternalOutput").ap()
        dbg_ws = nc.dram_tensor("dbg_ws", [128, 6144], BF16, kind="ExternalOutput").ap()
    elif stage == "rms0":
        dbg_x2 = nc.dram_tensor("dbg_x2", [D, 512], F32, kind="ExternalOutput").ap()
        dbg_r = nc.dram_tensor("dbg_r", [128, 512], F32, kind="ExternalOutput").ap()
        dbg_y2 = nc.dram_tensor("dbg_y2", [D, 512], BF16, kind="ExternalOutput").ap()
    elif stage == "ffnh1":
        dbg_h = nc.dram_tensor("dbg_h", [IFFN, 512], BF16, kind="ExternalOutput").ap()
    elif stage in ("q", "qk", "q_nodq"):
        dbg_q = nc.dram_tensor("dbg_q", [D, NT], BF16, kind="ExternalOutput").ap()
        kvc0 = layers[0]["kv_chunks"]
        dbg_K = nc.dram_tensor("dbg_K", [kvc0 * 128, NT], BF16, kind="ExternalOutput").ap()
    elif stage in ("qkv", "attn", "attn_nodq"):
        kvc0 = layers[0]["kv_chunks"]
        dbg_q = nc.dram_tensor("dbg_q", [D, NT], BF16, kind="ExternalOutput").ap()
        dbg_K = nc.dram_tensor("dbg_K", [4 * kvc0 * 128, NT], BF16, kind="ExternalOutput").ap()
        dbg_V = nc.dram_tensor("dbg_V", [4 * NT, kvc0 * 128], BF16, kind="ExternalOutput").ap()
    elif final:
        out_fm = nc.dram_tensor("out_fm", [D, NT], F32, kind="ExternalOutput").ap()
    else:
        xres_out = nc.dram_tensor("xres_out", [D, NT], F32, kind="ExternalOutput").ap()
        xw_out = nc.dram_tensor("xw_out", [D, NT], BF16, kind="ExternalOutput").ap()

    for cfg in layers:
        L = cfg["tag"]
        kvc = cfg["kv_chunks"]
        cfg["wq"] = inp(L + "wq", [D, D], F32)
        cfg["wk"] = inp(L + "wk", [D, kvc * 128], F32)
        cfg["wv"] = inp(L + "wv", [D, kvc * 128], F32)
        cfg["wo"] = inp(L + "wo", [D, D], F32)
        t["bq_" + L] = inp(L + "bq", [128, NH], F32)
        t["bk_" + L] = inp(L + "bk", [128, kvc], F32)
        t["bv_" + L] = inp(L + "bv", [1, kvc * 128], F32)
        if cfg["bo"] is not None:
            t["bo_" + L] = inp(L + "bo", [128, NKC], F32)
        if cfg["pe"]:
            t["pe_" + L] = inp(L + "pe", [128, NT], BF16)
        t[L + "gu_pk"] = inp(L + "gu_pk", [256, GU_SL], I32)
        t[L + "gu_sc"] = inp(L + "gu_sc", [D, GU_SL], F16)
        t[L + "dw_pk"] = inp(L + "dw_pk", [704, DW_SL], I32)
        t[L + "dw_sc"] = inp(L + "dw_sc", [IFFN, DW_SL], F16)
        t[L + "K_in"] = nc.dram_tensor(L + "K_in", [kvc * 128, NT], BF16).ap()
        t[L + "K_g"] = nc.dram_tensor(L + "K_g", [4 * kvc * 128, NT], BF16).ap()
        t[L + "V_in"] = nc.dram_tensor(L + "V_in", [NT, kvc * 128], BF16).ap()
        t[L + "V_g"] = nc.dram_tensor(L + "V_g", [4 * NT, kvc * 128], BF16).ap()
        t[L + "gu_in"] = nc.dram_tensor(L + "gu_in", [D, GU_SL], BF16).ap()
        t[L + "gu_w"] = nc.dram_tensor(L + "gu_w", [8 * D, GU_SL], BF16).ap()
        t[L + "dw_in"] = nc.dram_tensor(L + "dw_in", [IFFN, DW_SL], BF16).ap()
        t[L + "dw_w"] = nc.dram_tensor(L + "dw_w", [8 * IFFN, DW_SL], BF16).ap()
    t["jmaskA_in"] = inp("jmaskA", [128, 44], I32)
    t["jmaskB_in"] = inp("jmaskB", [128, 44], I32)
    t["masks_dram"] = masks_in

    with es:
        sb = lambda name, shape, dt: es.enter_context(nc.sbuf_tensor(name, shape, dt))
        ps = lambda name: es.enter_context(nc.psum_tensor(name, [128, 1024], F32))
        t["x_resid"] = sb("x_resid", [128, NKC * NT], F32).ap()
        XW = sb("x_work", [128, NKC * NT], BF16)
        t["x_work"] = XW.ap()
        t["masks"] = XW.ap()[:, 0:4096]
        t["probs"] = XW.ap()[:, 4096:6144]
        t["kring"] = XW.ap()[:, 6144:7168]
        t["vring"] = XW.ap()[:, 7168:8192]
        RING = 8
        t["RING"] = RING
        SCR = sb("SCR", [128, 2 * NKC * NT], BF16)
        t["qf"] = SCR.ap()[:, 0:NKC * NT]
        t["af"] = SCR.ap()[:, NKC * NT:2 * NKC * NT]
        t["h_sb"] = SCR.ap()[:, 0:44 * 512]
        t["y_sb"] = SCR.ap()[:, 44 * 512:44 * 512 + NKC * 512]
        afi = SCR.ap().bitcast(I32)
        aff = SCR.ap().bitcast(F32)
        afh = SCR.ap().bitcast(F16)
        base = 8192
        t["dq"] = {
            "psrc": afi[:, base + 0: base + 1408],
            "B": afi[:, base + 1408: base + 2112],
            "Bf": aff[:, base + 1408: base + 2112],
            "C": afi[:, base + 2112: base + 2816],
            "Derr": aff[:, base + 2816: base + 3520],
            "wt": SCR.ap()[:, 2 * (base + 3520): 2 * (base + 3520) + 1408],
            "sc": afh[:, 2 * (base + 4224): 2 * (base + 4224) + 1408],
        }
        WB = sb("wbig", [128, 6144], BF16)
        t["wcache"] = WB.ap()[:, 0:4096]
        t["wslab"] = WB.ap().bitcast(F32)[:, 2048:3072]
        t["wstream"] = WB.ap()[:, 0:6144]
        t["kstage"] = sb("kstage", [128, NT], BF16).ap()
        t["vstage"] = sb("vstage", [128, 512], BF16).ap()
        ROWS = sb("rows", [1, 3072], F32)
        t["rows"] = ROWS.ap()
        t["bvrow"] = ROWS.ap().bitcast(BF16)[0:1, 0:2048]
        t["bc_sb"] = sb("bc_sb", [128, 2048], F32).ap()
        t["statbuf"] = sb("statbuf", [128, 2048], BF16).ap()
        t["tmpf"] = sb("tmpf", [128, 1024], F32).ap()
        t["tmpf2"] = t["tmpf"][:, 512:1024]
        t["ones_k"] = sb("ones_k", [128, 1], BF16).ap()
        t["ones_1t"] = sb("ones_1t", [1, 128], BF16).ap()
        t["ones_1t_f"] = sb("ones_1t_f", [1, 128], F32).ap()
        t["sg_sb"] = sb("sg_sb", [128, 512], BF16).ap()
        t["biases"] = sb("biases", [128, 64], F32).ap()
        t["pe_own"] = sb("pe_own", [128, NT], BF16).ap()
        t["jmaskA"] = sb("jmaskA_sb", [128, 44], I32).ap()
        t["jmaskB"] = sb("jmaskB_sb", [128, 44], I32).ap()
        t["P"] = [ps("P%d" % i).ap() for i in range(4)]

        block = es.enter_context(nc.Block())
        kb = KB(nc, block, es)

        def _init(sync):
            for kc in range(NKC):
                kb.dma(sync, t["x_resid"][:, kc * NT:(kc + 1) * NT],
                       xin[kc * 128:(kc + 1) * 128, :], "ld_x")
                if entry_has_xw:
                    kb.dma(sync, t["x_work"][:, kc * NT:(kc + 1) * NT],
                           xwin[kc * 128:(kc + 1) * 128, :], "ld_x")
            kb.dma(sync, t["jmaskA"], t["jmaskA_in"][:, :], "ld_x")
            kb.dma(sync, t["jmaskB"], t["jmaskB_in"][:, :], "ld_x")
        block.sync(_init)
        ldx = kb.c("ld_x")
        def _mset(vector):
            kb.wait(vector, "ld_x", ldx)
            vector.memset(t["ones_k"], 1.0)
            vector.memset(t["ones_1t"], 1.0)
            i = vector.memset(t["ones_1t_f"], 1.0)
            kb.inc(i, "entry", 1)
        block.vector(_mset)

        for li, cfg in enumerate(layers):
            L = cfg["tag"]
            t["bq"], t["bk"] = t["bq_" + L], t["bk_" + L]
            t["bo"] = t.get("bo_" + L)
            t["K_in"], t["K_g"] = t[L + "K_in"], t[L + "K_g"]
            t["V_in"], t["V_g"] = t[L + "V_in"], t[L + "V_g"]
            bqs = t["biases"][:, 0:NH]
            bks = t["biases"][:, NH:NH + cfg["kv_chunks"]]
            bos = t["biases"][:, 32:48]
            def _lb2(sync, cfg=cfg, L=L, bqs=bqs, bks=bks, bos=bos):
                kb.dma(sync, bqs, t["bq_" + L][:, :], "ld_b")
                kb.dma(sync, bks, t["bk_" + L][:, :], "ld_b")
                if cfg["bo"] is not None:
                    kb.dma(sync, bos, t["bo_" + L][:, :], "ld_b")
                if cfg["pe"]:
                    kb.dma(sync, t["pe_own"], t["pe_" + L][:, :], "ld_b")
                kb.dma(sync, t["rows"][0:1, 1024:1024 + cfg["kv_chunks"] * 128],
                       t["bv_" + L][0:1, :], "ld_b")
            block.sync(_lb2)
            ldb = kb.c("ld_b")
            def _cvb(vector, cfg=cfg, ldb=ldb):
                kb.wait(vector, "ld_b", ldb)
                i = vector.tensor_copy(out=t["bvrow"][0:1, 0:cfg["kv_chunks"] * 128],
                                       in_=t["rows"][0:1, 1024:1024 + cfg["kv_chunks"] * 128])
                kb.inc(i, "entry", 1)
            block.vector(_cvb)
            t["bq"], t["bk"] = bqs, bks
            t["bo"] = bos if cfg["bo"] is not None else None
            if stage not in ("q_nodq", "attn_nodq"):
                emit_dequant(kb, t, cfg)
            if li == 0:
                kb.sem("xw_rdy")
                if not entry_has_xw:
                    def _cast(vector):
                        kb.wait(vector, "ld_x", ldx)
                        for kc in range(NKC):
                            i = vector.tensor_copy(out=t["x_work"][:, kc * NT:(kc + 1) * NT],
                                                   in_=t["x_resid"][:, kc * NT:(kc + 1) * NT])
                        kb.inc(i, "xw_rdy", 1)
                    block.vector(_cast)
                else:
                    def _xwv(vector):
                        kb.wait(vector, "ld_x", ldx)
                        kb.inc(vector.engine_nop(), "xw_rdy", 1)
                    block.vector(_xwv)
                cfg["entry_gate"] = ("xw_rdy", 1)
            if final and li == len(layers) - 1:
                cfg["store"] = out_fm
            cfg["stage"] = stage
            emit_layer(kb, t, cfg)

        last = layers[-1]["tag"]
        if stage == "rms0":
            L = layers[0]["tag"]
            apn = kb.c(L + "rms0_ap")
            def _dumpr(gpsimd):
                kb.wait(gpsimd, L + "rms0_ap", apn)
                for kc in range(NKC):
                    kb.dma(gpsimd, dbg_x2[kc * 128:(kc + 1) * 128, :],
                           t["x_resid"][:, kc * NT:kc * NT + 512], "st_out")
                    kb.dma(gpsimd, dbg_y2[kc * 128:(kc + 1) * 128, :],
                           t["y_sb"][:, kc * 512:(kc + 1) * 512], "st_out")
                kb.dma(gpsimd, dbg_r, t["bc_sb"][:, 0:512], "st_out")
                kb.wait(gpsimd, "st_out")
            block.gpsimd(_dumpr)
        elif stage in ("ffny", "ffny1"):
            L = layers[0]["tag"]
            yn = kb.c(L + "f_y")
            def _dumpy(gpsimd):
                kb.wait(gpsimd, L + "f_y", yn)
                for kc in range(NKC):
                    kb.dma(gpsimd, dbg_y[kc * 128:(kc + 1) * 128, :],
                           t["y_sb"][:, kc * 512:(kc + 1) * 512], "st_out")
                kb.dma(gpsimd, dbg_dw.opt(), t[L + "dw_w"].opt(), "st_out")
                for ic in range(44):
                    kb.dma(gpsimd, dbg_h[ic * 128:(ic + 1) * 128, :],
                           t["h_sb"][:, ic * 512:(ic + 1) * 512], "st_out")
                kb.dma(gpsimd, dbg_ws, t["wstream"][:, :], "st_out")
                kb.wait(gpsimd, "st_out")
            block.gpsimd(_dumpy)
        elif stage in ("ffnh", "ffnh1"):
            L = layers[0]["tag"]
            hn = kb.c(L + "f_h")
            def _dumph(gpsimd):
                kb.wait(gpsimd, L + "f_h", hn)
                for ic in range(44):
                    kb.dma(gpsimd, dbg_h[ic * 128:(ic + 1) * 128, :],
                           t["h_sb"][:, ic * 512:(ic + 1) * 512], "st_out")
                kb.wait(gpsimd, "st_out")
            block.gpsimd(_dumph)
        elif stage in ("q", "qk", "q_nodq"):
            L = layers[0]["tag"]
            qn = kb.c(L + "q_ready")
            kn = kb.c(L + "k_st")
            def _dumpq(gpsimd):
                kb.wait(gpsimd, L + "q_ready", qn)
                for kc in range(NKC):
                    kb.dma(gpsimd, dbg_q[kc * 128:(kc + 1) * 128, :],
                           t["qf"][:, kc * NT:(kc + 1) * NT], "st_out")
                if kn:
                    kb.wait(gpsimd, L + "k_st", kn)
                    kb.dma(gpsimd, dbg_K.opt(), t[L + "K_in"].opt(), "st_out")
                kb.wait(gpsimd, "st_out")
            block.gpsimd(_dumpq)
        elif stage == "dq":
            L = layers[0]["tag"]
            cgu, cdw = kb.c(L + "ccGU"), kb.c(L + "ccDW")
            def _dumpd(gpsimd):
                kb.wait(gpsimd, L + "ccGU", cgu)
                kb.wait(gpsimd, L + "ccDW", cdw)
                kb.dma(gpsimd, dbg_gu.opt(), t[L + "gu_w"].opt(), "st_out")
                kb.dma(gpsimd, dbg_dw.opt(), t[L + "dw_w"].opt(), "st_out")
                kb.wait(gpsimd, "st_out")
            block.gpsimd(_dumpd)
        elif stage in ("qkv", "attn", "attn_nodq"):
            L = layers[0]["tag"]
            src_t = t["qf"] if stage == "qkv" else t["af"]
            gsem = (L + "q_ready", kb.c(L + "q_ready")) if stage == "qkv" else \
                   (L + "a_afe", kb.c(L + "a_afe"))
            ck, cv2 = kb.c(L + "ccK"), kb.c(L + "ccV")
            def _dump(gpsimd):
                kb.wait(gpsimd, gsem[0], gsem[1])
                kb.wait(gpsimd, L + "ccK", ck)
                kb.wait(gpsimd, L + "ccV", cv2)
                for kc in range(NKC):
                    kb.dma(gpsimd, dbg_q[kc * 128:(kc + 1) * 128, :],
                           src_t[:, kc * NT:(kc + 1) * NT], "st_out")
                kb.dma(gpsimd, dbg_K.opt(), t["K_g"].opt(), "st_out")
                kb.dma(gpsimd, dbg_V.opt(), t["V_g"].opt(), "st_out")
                kb.wait(gpsimd, "st_out")
            block.gpsimd(_dump)
        elif stage == "wo":
            L = layers[0]["tag"]
            apl1 = kb.c(L + "ln1_ap")
            def _dumpw(gpsimd):
                kb.wait(gpsimd, L + "ln1_ap", apl1)
                for kc in range(NKC):
                    kb.dma(gpsimd, xres_out[kc * 128:(kc + 1) * 128, :],
                           t["x_resid"][:, kc * NT:(kc + 1) * NT], "st_out")
                    kb.dma(gpsimd, xw_out[kc * 128:(kc + 1) * 128, :],
                           t["x_work"][:, kc * NT:(kc + 1) * NT], "st_out")
                kb.wait(gpsimd, "st_out")
            block.gpsimd(_dumpw)
        elif final:
            c0 = kb.c(last + "ln2tg0_st")
            c1 = kb.c(last + "ln2tg1_st")
            def _out(gpsimd):
                kb.wait(gpsimd, last + "ln2tg0_st", c0)
                kb.wait(gpsimd, last + "ln2tg1_st", c1)
            block.gpsimd(_out)
        else:
            apc0 = kb.c(last + "ln2tg0_ap")
            apc = kb.c(last + "ln2tg1_ap")
            apl1 = kb.c(last + "ln1_ap")
            def _out(gpsimd):
                kb.wait(gpsimd, last + "ln1_ap", apl1)
                kb.wait(gpsimd, last + "ln2tg0_ap", apc0)
                kb.wait(gpsimd, last + "ln2tg1_ap", apc)
                for kc in range(NKC):
                    kb.dma(gpsimd, xres_out[kc * 128:(kc + 1) * 128, :],
                           t["x_resid"][:, kc * NT:(kc + 1) * NT], "st_out")
                    kb.dma(gpsimd, xw_out[kc * 128:(kc + 1) * 128, :],
                           t["x_work"][:, kc * NT:(kc + 1) * NT], "st_out")
                kb.wait(gpsimd, "st_out")
            block.gpsimd(_out)
    return nc, names


def _sinusoidal_pe(tmax, d):
    pos = np.arange(tmax, dtype=np.float32)[:, None]
    i = np.arange(0, d, 2, dtype=np.float32)
    inv = np.exp(-(np.log(10000.0) * i / d)).astype(np.float32)
    ang = pos * inv
    pe = np.zeros((tmax, d), np.float32)
    pe[:, 0::2] = np.sin(ang)
    pe[:, 1::2] = np.cos(ang)
    return pe


def _core_tokens(r):
    rr = r % 4
    tiles = [4 * l + rr for l in range(8)]
    return np.concatenate([np.arange(128 * ti, 128 * (ti + 1)) for ti in tiles]), tiles


def _masks_for_core(r):
    rr = r % 4
    m = np.zeros((128, 32 * 128), np.float32)
    for mm in range(8):
        for rho in range(4):
            blk = m[:, (mm * 4 + rho) * 128:(mm * 4 + rho + 1) * 128]
            if rho < rr:
                blk[:] = 1.0
            elif rho == rr:
                kk, qq = np.meshgrid(np.arange(128), np.arange(128), indexing="ij")
                blk[:] = (kk <= qq).astype(np.float32)
    return m.astype(BF)


def _ffn_host(layer_inputs, prefix):
    gu_p = np.asarray(layer_inputs[prefix + "gate_up_packed"])
    gu_s = np.asarray(layer_inputs[prefix + "gate_up_scales"])
    dw_p = np.asarray(layer_inputs[prefix + "down_packed"])
    dw_s = np.asarray(layer_inputs[prefix + "down_scales"])
    o_perm = np.concatenate([PI_I, IFFN + PI_I])
    guT = np.ascontiguousarray(gu_p[o_perm].T)
    g_of_h = (PI_H // 128).astype(np.int64)
    gu_sc = np.ascontiguousarray(gu_s[:, o_perm][g_of_h])
    dwT = np.ascontiguousarray(dw_p.T[:, PI_H])
    g_of_i = (PI_I // 128).astype(np.int64)
    dw_sc = np.ascontiguousarray(dw_s[:, PI_H][g_of_i])
    return guT, gu_sc, dwT, dw_sc


def kernel(**inputs):
    X = np.asarray(inputs["L0_attn_X"], np.float32)
    pe_t = _sinusoidal_pe(S, DH)

    cfg0 = dict(tag="A", kv_chunks=16, pe=False, bo=1)
    cfg1 = dict(tag="B", kv_chunks=4, pe=True, bo=None)

    com0 = {
        "Awq": np.ascontiguousarray(inputs["L0_attn_Wq"].T[PI_H], np.float32),
        "Awk": np.ascontiguousarray(inputs["L0_attn_Wk"].T[PI_H], np.float32),
        "Awv": np.ascontiguousarray(inputs["L0_attn_Wv"].T[PI_H], np.float32),
        "Awo": np.ascontiguousarray(np.asarray(inputs["L0_attn_Wo"]).T[:, PI_H], np.float32),
        "Abq": np.ascontiguousarray(np.asarray(inputs["L0_attn_bq"]).reshape(16, 128).T, np.float32),
        "Abk": np.ascontiguousarray(np.asarray(inputs["L0_attn_bk"]).reshape(16, 128).T, np.float32),
        "Abv": np.asarray(inputs["L0_attn_bv"], np.float32)[None, :].copy(),
        "Abo": np.ascontiguousarray(np.asarray(inputs["L0_attn_bo"])[PI_H].reshape(16, 128).T, np.float32),
    }
    jj = (128 * np.arange(44)[None, :] + np.arange(128)[:, None]) // 704
    ja_ = (128 * np.arange(44)) // 704
    com0["jmaskA"] = np.where(jj == ja_[None, :], -1, 0).astype(np.int32)
    com0["jmaskB"] = np.where(jj != ja_[None, :], -1, 0).astype(np.int32)
    kcols = np.concatenate([np.arange(256 * g, 256 * g + 128) for g in range(4)])
    vcols = np.concatenate([np.arange(256 * g + 128, 256 * (g + 1)) for g in range(4)])
    wkv = np.asarray(inputs["L1_attn_wkv"], np.float32)
    bkv = np.asarray(inputs["L1_attn_bkv"], np.float32)
    com1 = {
        "Bwq": np.ascontiguousarray(np.asarray(inputs["L1_attn_wq"], np.float32)[PI_H]),
        "Bwk": np.ascontiguousarray(wkv[PI_H][:, kcols]),
        "Bwv": np.ascontiguousarray(wkv[PI_H][:, vcols]),
        "Bwo": np.ascontiguousarray(np.asarray(inputs["L1_attn_wo"], np.float32)[:, PI_H]),
        "Bbq": np.ascontiguousarray(np.asarray(inputs["L1_attn_bq"], np.float32).reshape(16, 128).T),
        "Bbk": np.ascontiguousarray(bkv[kcols].reshape(4, 128).T),
        "Bbv": bkv[vcols][None, :].copy(),
        "jmaskA": com0["jmaskA"], "jmaskB": com0["jmaskB"],
    }
    guT0, gusc0, dwT0, dwsc0 = _ffn_host(inputs, "L0_ffn_")
    guT1, gusc1, dwT1, dwsc1 = _ffn_host(inputs, "L1_ffn_")

    per_core0, per_core1, toks = [], [], []
    for r in range(8):
        b = r // 4
        tok, _ = _core_tokens(r)
        toks.append(tok)
        masks = _masks_for_core(r)
        sl = slice(GU_SL * r, GU_SL * (r + 1))
        sld = slice(DW_SL * r, DW_SL * (r + 1))
        pc0 = {
            "xin": np.ascontiguousarray(X[b].T[PI_H][:, tok], np.float32),
            "masks": masks,
            "Agu_pk": np.ascontiguousarray(guT0[:, sl]),
            "Agu_sc": np.ascontiguousarray(gusc0[:, sl]),
            "Adw_pk": np.ascontiguousarray(dwT0[:, sld]),
            "Adw_sc": np.ascontiguousarray(dwsc0[:, sld]),
        }
        pc1 = {
            "masks": masks,
            "Bpe": np.ascontiguousarray(pe_t[tok % S].T).astype(BF),
            "Bgu_pk": np.ascontiguousarray(guT1[:, sl]),
            "Bgu_sc": np.ascontiguousarray(gusc1[:, sl]),
            "Bdw_pk": np.ascontiguousarray(dwT1[:, sld]),
            "Bdw_sc": np.ascontiguousarray(dwsc1[:, sld]),
        }
        per_core0.append(pc0)
        per_core1.append(pc1)

    nc0, _ = build_launch([cfg0], final=False, entry_has_xw=False)
    in_maps0 = [{**com0, **pc} for pc in per_core0]
    res0 = run_bass_kernel_spmd(nc0, in_maps0, core_ids=list(range(8)))
    cfg1 = dict(tag="B", kv_chunks=4, pe=True, bo=None)
    nc1, _ = build_launch([cfg1], final=True, entry_has_xw=True)
    in_maps1 = []
    for r in range(8):
        in_maps1.append({**com1, **per_core1[r],
                         "xin": res0.results[r]["xres_out"],
                         "xwin": res0.results[r]["xw_out"]})
    res1 = run_bass_kernel_spmd(nc1, in_maps1, core_ids=list(range(8)))

    out = np.empty((2, S, D), np.float32)
    inv = np.empty(D, np.int64)
    inv[PI_H] = np.arange(D)
    for r in range(8):
        b = r // 4
        o = res1.results[r]["out_fm"]
        out[b, toks[r], :] = o[inv].T
    return out


# version 04
# speedup vs baseline: 1.0102x; 1.0102x over previous
import contextlib
import numpy as np
import ml_dtypes
import concourse.bass as bass
import concourse.mybir as mybir
from concourse.bass_utils import run_bass_kernel_spmd

F32 = mybir.dt.float32
BF16 = mybir.dt.bfloat16
F16 = mybir.dt.float16
I32 = mybir.dt.int32
I8 = mybir.dt.int8
AF = mybir.ActivationFunctionType
ALU = mybir.AluOpType
f32 = np.float32
BF = ml_dtypes.bfloat16

D = 2048
S = 4096
NT = 1024
NH = 16
DH = 128
NKC = 16
IFFN = 5632
OGU = 2 * IFFN
C_EXP = 15.0
SCL = float(1.0 / np.sqrt(128.0))
LN_EPS = 1e-5
RMS_EPS = 1e-6
GU_SL = OGU // 8
DW_SL = D // 8

PI_H = np.array([8 * c + j for j in range(8) for c in range(256)], dtype=np.int64)
PI_I = np.array([8 * c + j for j in range(8) for c in range(704)], dtype=np.int64)


class KB:
    def __init__(self, nc, block, es):
        self.nc, self.block, self.es = nc, block, es
        self._h, self._c = {}, {}

    @staticmethod
    def _cn(name):
        return name[1:] if name[:1] in ("A", "B") else name

    def sem(self, name):
        name = self._cn(name)
        if name not in self._h:
            self._h[name] = self.es.enter_context(self.nc.semaphore(name))
            self._c[name] = 0
        return self._h[name]

    def dma(self, eng, out, in_, sname):
        sname = self._cn(sname)
        h = self.sem(sname)
        eng.dma_start(out=out, in_=in_).then_inc(h, 16)
        self._c[sname] += 16
        return self._c[sname]

    def inc(self, inst, sname, n=1):
        sname = self._cn(sname)
        inst.then_inc(self.sem(sname), n)
        self._c[sname] += n
        return self._c[sname]

    def c(self, sname):
        sname = self._cn(sname)
        self.sem(sname)
        return self._c[sname]

    def wait(self, eng, sname, val=None):
        sname = self._cn(sname)
        eng.wait_ge(self.sem(sname), self._c[sname] if val is None else val)



def emit_layer(kb, t, cfg):
    nc, block = kb.nc, kb.block
    L = cfg["tag"]
    kvc = cfg["kv_chunks"]
    pe = cfg["pe"]
    P = t["P"]

    x_resid, x_work = t["x_resid"], t["x_work"]
    qf = t["qf"]
    af = t["af"]
    wcache, wslab = t["wcache"], t["wslab"]

    def proj_fm(tag, w_ap, n_oc, evac, rhs=None, war=None, gate_mm=None,
                psum_war=None, slab_war=None):
        if rhs is None:
            rhs = x_work
        s_w, s_cv, s_ev = tag + "_w", tag + "_cv", tag + "_ev"
        kb.sem(s_ev); kb.sem("slabcv")
        n_grp = (n_oc + 1) // 2
        for g in range(n_grp):
            oc0 = g * 2
            ocn = min(2, n_oc - oc0)
            wcols = ocn * 128
            cvm = []
            for kc in range(NKC):
                slabv = max(0, kb.c("slabcv") - 1)
                sw = "slab%d" % (kc % 2)
                def _w(sync, v=slabv, kc=kc, oc0=oc0, wcols=wcols, sw=sw, g=g):
                    if g == 0 and kc == 0 and slab_war is not None:
                        kb.wait(sync, slab_war[0], slab_war[1])
                    kb.wait(sync, "slabcv", v)
                    kb.dma(sync, wslab[:, (kc % 2) * 512:(kc % 2) * 512 + wcols],
                           w_ap[kc * 128:(kc + 1) * 128, oc0 * 128:oc0 * 128 + wcols], sw)
                block.sync(_w)
                wv = kb.c(sw)
                def _cv(vector, kc=kc, wv=wv, wcols=wcols, g=g, sw=sw):
                    kb.wait(vector, sw, wv)
                    if kc == 0 and g > 0:
                        kb.wait(vector, tag + "_mm", kb.c(tag + "_mm"))
                    if kc == 0 and g == 0 and war is not None:
                        kb.wait(vector, war[0], war[1])
                    i = vector.tensor_copy(out=wcache[:, kc * wcols:(kc + 1) * wcols],
                                           in_=wslab[:, (kc % 2) * 512:(kc % 2) * 512 + wcols])
                    kb.inc(i, s_cv, 1)
                    kb.inc(vector.engine_nop(), "slabcv", 1)
                block.vector(_cv)
                cvm.append(kb.c(s_cv))
            ev_before = kb.c(s_ev)
            def _mm(tensor, g=g, ocn=ocn, wcols=wcols, cvm=cvm, ev_before=ev_before):
                if g > 0:
                    kb.wait(tensor, s_ev, ev_before)
                if g == 0 and gate_mm is not None:
                    kb.wait(tensor, gate_mm[0], gate_mm[1])
                if g == 0 and psum_war is not None:
                    kb.wait(tensor, psum_war[0], psum_war[1])
                last = None
                for kc in range(NKC):
                    kb.wait(tensor, s_cv, cvm[kc])
                    for oi in range(ocn):
                        for tg in range(2):
                            last = tensor.matmul(
                                P[oi][:, tg * 512:(tg + 1) * 512],
                                wcache[:, kc * wcols + oi * 128:kc * wcols + (oi + 1) * 128],
                                rhs[:, kc * NT + tg * 512:kc * NT + (tg + 1) * 512],
                                start=(kc == 0), stop=(kc == NKC - 1))
                kb.inc(last, tag + "_mm", 1)
            block.tensor(_mm)
            mmc = kb.c(tag + "_mm")
            for oi in range(ocn):
                evac(oc0 + oi, P[oi], mmc)

    if cfg.get("stage") == "dq":
        return
    def evac_q(oc, ps, mmc):
        def _(vector, oc=oc, ps=ps, mmc=mmc):
            kb.wait(vector, L + "q_mm", mmc)
            if pe:
                i = vector.scalar_tensor_tensor(
                    out=qf[:, oc * NT:(oc + 1) * NT], in0=ps[:, :],
                    scalar=t["bq"][:, oc:oc + 1], in1=t["pe_own"][:, :],
                    op0=ALU.add, op1=ALU.add)
            else:
                i = vector.tensor_scalar(
                    out=qf[:, oc * NT:(oc + 1) * NT], in0=ps[:, :],
                    scalar1=t["bq"][:, oc:oc + 1], scalar2=None, op0=ALU.add)
            kb.inc(i, L + "q_ev", 1)
            kb.inc(vector.engine_nop(), L + "q_ready", 1)
        block.vector(_)
    kb.sem(L + "q_ready")
    proj_fm(L + "q", cfg["wq"], NH, evac_q, gate_mm=cfg.get("entry_gate"),
            war=cfg.get("q_war"), slab_war=cfg.get("q_war"))
    if cfg.get("stage") in ("q", "q_nodq"):
        return

    kst = t["kstage"]
    def evac_k(oc, ps, mmc):
        def _(vector, oc=oc, ps=ps, mmc=mmc):
            kb.wait(vector, L + "k_mm", mmc)
            if oc >= 1:
                kb.wait(vector, L + "k_st", kb.c(L + "k_st"))
            sl = kst[:, 0:NT]
            if pe:
                i = vector.scalar_tensor_tensor(
                    out=sl, in0=ps[:, :], scalar=t["bk"][:, oc:oc + 1],
                    in1=t["pe_own"][:, :], op0=ALU.add, op1=ALU.add)
            else:
                i = vector.tensor_scalar(out=sl, in0=ps[:, :],
                                         scalar1=t["bk"][:, oc:oc + 1], scalar2=None,
                                         op0=ALU.add)
            kb.inc(i, L + "k_evd", 1)
            kb.inc(vector.engine_nop(), L + "k_ev", 1)
        block.vector(_)
        evd = kb.c(L + "k_evd")
        def _d(gpsimd, oc=oc, evd=evd):
            kb.wait(gpsimd, L + "k_evd", evd)
            kb.dma(nc.gpsimd, t["K_in"][oc * 128:(oc + 1) * 128, :],
                   kst[:, 0:NT], L + "k_st")
        block.gpsimd(_d)
    kb.sem(L + "k_st"); kb.sem(L + "k_evd")
    proj_fm(L + "k", cfg["wk"], kvc, evac_k, war=(L + "q_mm", kb.c(L + "q_mm")),
            psum_war=(L + "q_ev", kb.c(L + "q_ev")))
    if cfg.get("stage") == "qk":
        return

    VW = 256
    vcols = kvc * 128
    n_vg = vcols // VW
    vst = t["vstage"]
    kb.sem(L + "v_st"); kb.sem(L + "v_cv"); kb.sem(L + "v_ev"); kb.sem(L + "v_mm")
    for vg in range(n_vg):
        vw = VW
        cvm = []
        for kc in range(NKC):
            slabv = max(0, kb.c("slabcv") - 1)
            sw = "slab%d" % (kc % 2)
            def _w(sync, v=slabv, kc=kc, vg=vg, vw=vw, sw=sw):
                kb.wait(sync, "slabcv", v)
                kb.dma(sync, wslab[:, (kc % 2) * 512:(kc % 2) * 512 + vw],
                       cfg["wv"][kc * 128:(kc + 1) * 128, vg * VW:vg * VW + vw], sw)
            block.sync(_w)
            wv = kb.c(sw)
            def _cv(vector, kc=kc, wv=wv, vg=vg, vw=vw, sw=sw):
                kb.wait(vector, sw, wv)
                if kc == 0:
                    kb.wait(vector, L + "v_mm", kb.c(L + "v_mm"))
                    if vg == 0:
                        kb.wait(vector, L + "k_mm", kb.c(L + "k_mm"))
                i = vector.tensor_copy(out=wcache[:, kc * VW:kc * VW + vw],
                                       in_=wslab[:, (kc % 2) * 512:(kc % 2) * 512 + vw])
                kb.inc(i, L + "v_cv", 1)
                kb.inc(vector.engine_nop(), "slabcv", 1)
            block.vector(_cv)
            cvm.append(kb.c(L + "v_cv"))
        for tt in range(8):
            pid = tt % 4
            ev_before = kb.c(L + "v_ev")
            entc = kb.c("entry")
            kevc = kb.c(L + "k_ev")
            def _mm(tensor, tt=tt, pid=pid, vg=vg, vw=vw, cvm=cvm, ev_before=ev_before,
                    entc=entc, kevc=kevc):
                if tt == 0 and vg == 0:
                    kb.wait(tensor, "entry", entc)
                    kb.wait(tensor, L + "k_ev", kevc)
                if tt >= 4 or vg > 0:
                    kb.wait(tensor, L + "v_ev", max(0, ev_before - 3))
                for kc in range(NKC):
                    kb.wait(tensor, L + "v_cv", cvm[kc])
                    tensor.matmul(P[pid][:, 0:vw],
                                  x_work[:, kc * NT + tt * 128:kc * NT + (tt + 1) * 128],
                                  wcache[:, kc * VW:kc * VW + vw],
                                  start=(kc == 0), stop=False)
                i = tensor.matmul(P[pid][:, 0:vw], t["ones_1t"][0:1, :],
                                  t["bvrow"][0:1, vg * VW:vg * VW + vw],
                                  start=False, stop=True)
                kb.inc(i, L + "v_mm", 1)
            block.tensor(_mm)
            mmc = kb.c(L + "v_mm")
            def _ev(vector, tt=tt, pid=pid, vg=vg, vw=vw, mmc=mmc):
                kb.wait(vector, L + "v_mm", mmc)
                kb.wait(vector, L + "v_st", kb.c(L + "v_st"))
                i = vector.tensor_copy(out=vst[:, 0:vw], in_=P[pid][:, 0:vw])
                kb.inc(i, L + "v_evd", 1)
                kb.inc(vector.engine_nop(), L + "v_ev", 1)
            kb.sem(L + "v_evd")
            block.vector(_ev)
            evd = kb.c(L + "v_evd")
            def _d(gpsimd, tt=tt, vg=vg, vw=vw, evd=evd):
                kb.wait(gpsimd, L + "v_evd", evd)
                kb.dma(nc.gpsimd, t["V_in"][tt * 128:(tt + 1) * 128, vg * VW:vg * VW + vw],
                       vst[:, 0:vw], L + "v_st")
            block.gpsimd(_d)

    kstc, vstc = kb.c(L + "k_st"), kb.c(L + "v_st")
    def _cc(gpsimd):
        kb.wait(gpsimd, L + "k_st", kstc)
        kb.wait(gpsimd, L + "v_st", vstc)
        i = gpsimd.collective_compute(
            "AllGather", ALU.bypass, replica_groups=[[0, 1, 2, 3], [4, 5, 6, 7]],
            ins=[t["K_in"].opt()], outs=[t["K_g"].opt()])
        kb.inc(i, L + "ccK", 1)
        i = gpsimd.collective_compute(
            "AllGather", ALU.bypass, replica_groups=[[0, 1, 2, 3], [4, 5, 6, 7]],
            ins=[t["V_in"].opt()], outs=[t["V_g"].opt()])
        kb.inc(i, L + "ccV", 1)
    block.gpsimd(_cc)

    if cfg.get("stage") == "qkv":
        return
    kring, vring = t["kring"], t["vring"]
    RING = t["RING"]
    probs = t["probs"]
    s_kv = L + "a_kv"; s_cons = L + "a_cons"; s_exp = L + "a_exp"
    s_msk = L + "a_msk"; s_av = L + "a_av"; s_z = L + "a_z"; s_afe = L + "a_afe"
    for nm in (s_kv, s_cons, s_exp, s_msk, s_av, s_z, s_afe):
        kb.sem(nm)
    kblk = kvc * 128
    q_ready = kb.c(L + "q_ready")
    ccK, ccV = kb.c(L + "ccK"), kb.c(L + "ccV")
    vmm_all = kb.c(L + "v_mm")
    def _ldm(sync):
        kb.wait(sync, L + "v_mm", vmm_all)
        kb.dma(sync, t["masks"], t["masks_dram"][:, :], L + "a_mload")
    kb.sem(L + "a_mload")
    block.sync(_ldm)
    mload = kb.c(L + "a_mload")

    ld_i = 0
    head_order = list(range(NH))
    first_h = head_order[0]
    for hpos, h in enumerate(head_order):
        kh = h // (NH // kvc) if kvc < NH else h
        for kap in range(32):
            m = kap // 4
            c0 = 128 * m
            slot = ld_i % RING
            skv = L + "a_kv%d" % slot
            cons_need = max(0, kb.c(s_av) - RING + 1)
            def _ld(sync, kap=kap, kh=kh, slot=slot, cons_need=cons_need, skv=skv):
                if cons_need:
                    kb.wait(sync, s_av, cons_need)
                if kap == 0 and h == first_h:
                    kb.wait(sync, L + "ccK", ccK)
                    kb.wait(sync, L + "ccV", ccV)
                kb.dma(sync, kring[:, slot * 128:(slot + 1) * 128],
                       t["K_g"][kblk * (kap % 4) + 128 * kh: kblk * (kap % 4) + 128 * (kh + 1),
                                128 * (kap // 4):128 * (kap // 4 + 1)], skv)
                kb.dma(sync, vring[:, slot * 128:(slot + 1) * 128],
                       t["V_g"][1024 * (kap % 4) + 128 * (kap // 4):
                                1024 * (kap % 4) + 128 * (kap // 4) + 128,
                                128 * kh:128 * (kh + 1)], skv)
            block.sync(_ld)
            ldc = kb.c(skv)
            ld_i += 1

            stp = P[kap % 2]
            pieces = [(c0, 512), (512, 1024)] if c0 < 512 else [(c0, 1024)]
            expc_need = kb.c(s_exp)
            def _st(tensor, kap=kap, h=h, slot=slot, stp=stp, pieces=pieces,
                    ldc=ldc, expc_need=expc_need, skv=skv):
                kb.wait(tensor, skv, ldc)
                if h == first_h and kap == 0:
                    kb.wait(tensor, L + "q_ready", q_ready)
                    kb.wait(tensor, L + "v_ev", kb.c(L + "v_ev"))
                if kap >= 2 or hpos > 0:
                    kb.wait(tensor, s_exp, max(0, expc_need - 1))
                last = None
                for (a, b) in pieces:
                    last = tensor.matmul(stp[:, a:b],
                                         kring[:, slot * 128:(slot + 1) * 128],
                                         qf[:, h * NT + a:h * NT + b],
                                         start=True, stop=True)
                kb.inc(last, L + "a_st", 1)
            kb.sem(L + "a_st")
            block.tensor(_st)
            stc = kb.c(L + "a_st")
            def _ex(scalar, kap=kap, stp=stp, c0=c0):
                kb.wait(scalar, L + "a_st", stc)
                if kap >= 2 or hpos > 0:
                    kb.wait(scalar, s_av, max(0, kb.c(s_av) - 1))
                i = scalar.activation(
                    out=probs[:, (kap % 2) * NT + c0:(kap % 2) * NT + NT],
                    in_=stp[:, c0:NT], func=AF.Exp, scale=f32(SCL), bias=f32(-C_EXP))
                kb.inc(i, s_exp, 1)
            block.scalar(_ex)
            expc = kb.c(s_exp)
            rho = kap % 4
            def _mk(vector, kap=kap, c0=c0, m=m, rho=rho):
                kb.wait(vector, s_exp, expc)
                if h == first_h and kap == 0:
                    kb.wait(vector, L + "a_mload", mload)
                i = vector.tensor_tensor(
                    out=probs[:, (kap % 2) * NT + c0:(kap % 2) * NT + c0 + 128],
                    in0=probs[:, (kap % 2) * NT + c0:(kap % 2) * NT + c0 + 128],
                    in1=t["masks"][:, (m * 4 + rho) * 128:(m * 4 + rho + 1) * 128],
                    op=ALU.mult)
                kb.inc(i, s_msk, 1)
            block.vector(_mk)
            mskc = kb.c(s_msk)
            def _av(tensor, kap=kap, slot=slot, pieces=pieces, mskc=mskc, h=h):
                kb.wait(tensor, s_msk, mskc)
                if kap == 0 and h == first_h:
                    kb.wait(tensor, "entry", 1)
                if kap == 0:
                    kb.wait(tensor, s_afe, kb.c(s_afe))
                for (a, b) in pieces:
                    tensor.matmul(P[3][0:1, a:b], t["ones_k"][:, 0:1],
                                  probs[:, (kap % 2) * NT + a:(kap % 2) * NT + b],
                                  start=(kap == 0), stop=(kap == 31),
                                  skip_group_check=True)
                last = None
                for (a, b) in pieces:
                    last = tensor.matmul(P[2][:, a:b],
                                         vring[:, slot * 128:(slot + 1) * 128],
                                         probs[:, (kap % 2) * NT + a:(kap % 2) * NT + b],
                                         start=(kap == 0), stop=(kap == 31),
                                         skip_group_check=True)
                kb.inc(last, s_av, 1)
            block.tensor(_av)
        avc = kb.c(s_av)
        def _zr(vector, h=h, avc=avc):
            kb.wait(vector, s_av, avc)
            i = vector.reciprocal(out=t["rows"][0:1, 0:1024], in_=P[3][0:1, 0:1024])
            kb.inc(i, s_z, 1)
        block.vector(_zr)
        zc = kb.c(s_z)
        def _bc(tensor, h=h, zc=zc):
            kb.wait(tensor, s_z, zc)
            last = None
            for tg in range(2):
                last = tensor.matmul(P[3][:, tg * 512:(tg + 1) * 512],
                                     t["ones_1t_f"][0:1, :],
                                     t["rows"][0:1, tg * 512:(tg + 1) * 512],
                                     start=True, stop=True)
            kb.inc(last, L + "a_bc", 1)
        kb.sem(L + "a_bc")
        block.tensor(_bc)
        bcc = kb.c(L + "a_bc")
        def _ae(vector, h=h, bcc=bcc):
            kb.wait(vector, L + "a_bc", bcc)
            if h == first_h:
                kb.wait(vector, L + "dq_done", kb.c(L + "dq_done"))
            vector.tensor_copy(out=t["bc_sb"][:, 0:1024], in_=P[3][:, 0:1024])
            i = vector.tensor_tensor(out=af[:, h * NT:(h + 1) * NT],
                                     in0=P[2][:, 0:1024], in1=t["bc_sb"][:, 0:1024],
                                     op=ALU.mult)
            kb.inc(i, s_afe, 1)
        block.vector(_ae)

    if cfg.get("stage") in ("attn", "attn_nodq"):
        return
    afe_all = kb.c(s_afe)
    def evac_wo(oc, ps, mmc):
        def _(vector, oc=oc, ps=ps, mmc=mmc):
            kb.wait(vector, L + "o_mm", mmc)
            if cfg["bo"] is not None:
                i = vector.scalar_tensor_tensor(
                    out=x_resid[:, oc * NT:(oc + 1) * NT], in0=ps[:, :],
                    scalar=t["bo"][:, oc:oc + 1], in1=x_resid[:, oc * NT:(oc + 1) * NT],
                    op0=ALU.add, op1=ALU.add)
            else:
                i = vector.tensor_tensor(
                    out=x_resid[:, oc * NT:(oc + 1) * NT], in0=ps[:, :],
                    in1=x_resid[:, oc * NT:(oc + 1) * NT], op=ALU.add)
            kb.inc(i, L + "o_ev", 1)
            kb.inc(vector.engine_nop(), L + "o_res", 1)
        block.vector(_)
    kb.sem(L + "o_res")
    def _gate(tensor):
        kb.wait(tensor, s_afe, afe_all)
    block.tensor(_gate)
    proj_fm(L + "o", cfg["wo"], NKC, evac_wo, rhs=af, war=(L + "v_mm", kb.c(L + "v_mm")),
            psum_war=(L + "a_afe", kb.c(L + "a_afe")))

    emit_ln(kb, t, L + "ln1", src=x_resid, out_bf=x_work, cols=NT,
            gate=(L + "o_res", kb.c(L + "o_res")))

    if cfg.get("stage") == "wo":
        return
    emit_ffn(kb, t, cfg)
    if cfg.get("stage") in ("ffnh", "ffny", "ffny1", "ffnh1", "rms0"):
        return


def emit_ln(kb, t, tag, src, out_bf, cols, col0=0, gate=None, store=None):
    nc, block = kb.nc, kb.block
    P = t["P"]
    sb = t["statbuf"]
    rows = t["rows"]
    s_sq = tag + "_sq"; s_mm = tag + "_mm"; s_row = tag + "_row"; s_bc = tag + "_bc"
    s_ap = tag + "_ap"
    for nm in (s_sq, s_mm, s_row, s_bc, s_ap):
        kb.sem(nm)
    npc = (cols + 511) // 512
    for kc in range(NKC):
        def _cast(vector, kc=kc):
            if kc == 0 and gate is not None:
                kb.wait(vector, gate[0], gate[1])
            if kc >= 1:
                kb.wait(vector, s_mm, kb.c(s_mm))
            vector.tensor_copy(out=sb[:, 0:cols],
                               in_=src[:, kc * NT + col0:kc * NT + col0 + cols])
            kb.inc(vector.engine_nop(), s_sq, 1)
        block.vector(_cast)
        def _sqr(scalar, kc=kc):
            if kc == 0 and gate is not None:
                kb.wait(scalar, gate[0], gate[1])
            if kc >= 1:
                kb.wait(scalar, s_mm, kb.c(s_mm))
            i = scalar.activation(out=sb[:, 1024:1024 + cols],
                                  in_=src[:, kc * NT + col0:kc * NT + col0 + cols],
                                  func=AF.Square)
            kb.inc(i, s_sq, 1)
        block.scalar(_sqr)
        sqc = kb.c(s_sq)
        def _mm(tensor, kc=kc, sqc=sqc):
            if kc == 0 and gate is not None:
                kb.wait(tensor, gate[0], gate[1])
            if kc == 0:
                kb.wait(tensor, "entry", 1)
            kb.wait(tensor, s_sq, sqc)
            last = None
            for pc in range(npc):
                a, b = pc * 512, min(cols, (pc + 1) * 512)
                tensor.matmul(P[0][0:1, a:b], t["ones_k"][:, 0:1],
                              sb[:, a:b], start=(kc == 0), stop=(kc == NKC - 1))
                last = tensor.matmul(P[1][0:1, a:b], t["ones_k"][:, 0:1],
                                     sb[:, 1024 + a:1024 + b],
                                     start=(kc == 0), stop=(kc == NKC - 1))
            kb.inc(last, s_mm, 1)
        block.tensor(_mm)
    mmc = kb.c(s_mm)
    def _row(vector, mmc=mmc):
        kb.wait(vector, s_mm, mmc)
        vector.tensor_scalar(out=rows[0:1, 0:cols], in0=P[0][0:1, 0:cols],
                             scalar1=1.0 / 2048.0, scalar2=None, op0=ALU.mult)
        vector.tensor_scalar(out=rows[0:1, 1024:1024 + cols], in0=P[1][0:1, 0:cols],
                             scalar1=1.0 / 2048.0, scalar2=None, op0=ALU.mult)
        vector.tensor_tensor(out=rows[0:1, 2048:2048 + cols], in0=rows[0:1, 0:cols],
                             in1=rows[0:1, 0:cols], op=ALU.mult)
        i = vector.tensor_tensor(out=rows[0:1, 1024:1024 + cols],
                                 in0=rows[0:1, 1024:1024 + cols],
                                 in1=rows[0:1, 2048:2048 + cols], op=ALU.subtract)
        kb.inc(i, s_row, 1)
    block.vector(_row)
    rc = kb.c(s_row)
    def _rstd(scalar, rc=rc):
        kb.wait(scalar, s_row, rc)
        scalar.activation(out=rows[0:1, 2048:2048 + cols],
                          in_=rows[0:1, 1024:1024 + cols], func=AF.Ln,
                          bias=f32(LN_EPS))
        i = scalar.activation(out=rows[0:1, 1024:1024 + cols],
                              in_=rows[0:1, 2048:2048 + cols], func=AF.Exp,
                              scale=f32(-0.5))
        kb.inc(i, s_row, 1)
    block.scalar(_rstd)
    rc2 = kb.c(s_row)
    def _fin(vector, rc2=rc2):
        kb.wait(vector, s_row, rc2)
        i = vector.tensor_tensor(out=rows[0:1, 2048:2048 + cols],
                                 in0=rows[0:1, 0:cols],
                                 in1=rows[0:1, 1024:1024 + cols], op=ALU.mult)
        kb.inc(i, s_row, 1)
    block.vector(_fin)
    rc3 = kb.c(s_row)
    def _bc(tensor, rc3=rc3):
        kb.wait(tensor, s_row, rc3)
        last = None
        for pc in range(npc):
            a, b = pc * 512, min(cols, (pc + 1) * 512)
            tensor.matmul(P[0][:, a:b], t["ones_1t_f"][0:1, :],
                          rows[0:1, 1024 + a:1024 + b], start=True, stop=True)
            last = tensor.matmul(P[1][:, a:b], t["ones_1t_f"][0:1, :],
                                 rows[0:1, 2048 + a:2048 + b], start=True, stop=True)
        kb.inc(last, s_bc, 1)
    block.tensor(_bc)
    bcc = kb.c(s_bc)
    def _bcev(vector, bcc=bcc):
        kb.wait(vector, s_bc, bcc)
        vector.tensor_copy(out=t["bc_sb"][:, 0:cols], in_=P[0][:, 0:cols])
        i = vector.tensor_copy(out=t["bc_sb"][:, 1024:1024 + cols], in_=P[1][:, 0:cols])
        kb.inc(i, s_bc, 1)
    block.vector(_bcev)
    bcc2 = kb.c(s_bc)
    for kc in range(NKC):
        def _ap(vector, kc=kc):
            kb.wait(vector, s_bc, bcc2)
            vector.tensor_tensor(out=t["tmpf"][:, 0:cols],
                                 in0=src[:, kc * NT + col0:kc * NT + col0 + cols],
                                 in1=t["bc_sb"][:, 0:cols], op=ALU.mult)
            if store is None:
                dst = src[:, kc * NT + col0:kc * NT + col0 + cols]
            else:
                kb.wait(vector, tag + "_st", kb.c(tag + "_st"))
                dst = t["tmpf2"][:, 0:cols]
            i = vector.tensor_tensor(out=dst,
                                     in0=t["tmpf"][:, 0:cols],
                                     in1=t["bc_sb"][:, 1024:1024 + cols],
                                     op=ALU.subtract)
            kb.inc(i, tag + "_apv", 1)
        kb.sem(tag + "_st"); kb.sem(tag + "_apv")
        block.vector(_ap)
        apvn = kb.c(tag + "_apv")
        if store is not None:
            def _std(gpsimd, kc=kc, apvn=apvn):
                kb.wait(gpsimd, tag + "_apv", apvn)
                kb.dma(gpsimd, store[kc * 128:(kc + 1) * 128, col0:col0 + cols],
                       t["tmpf2"][:, 0:cols], tag + "_st")
            block.gpsimd(_std)
            def _inc(scalar, apvn=apvn):
                kb.wait(scalar, tag + "_apv", apvn)
                kb.inc(scalar.nop(), s_ap, 1)
            block.scalar(_inc)
        else:
            def _cast(scalar, kc=kc, apvn=apvn):
                kb.wait(scalar, tag + "_apv", apvn)
                i = scalar.activation(out=out_bf[:, kc * NT + col0:kc * NT + col0 + cols],
                                      in_=src[:, kc * NT + col0:kc * NT + col0 + cols],
                                      func=AF.Copy, bias=0.0)
                kb.inc(i, s_ap, 1)
            block.scalar(_cast)


def emit_ffn(kb, t, cfg):
    nc, block = kb.nc, kb.block
    L = cfg["tag"]
    P = t["P"]
    x_resid, x_work = t["x_resid"], t["x_work"]
    h_sb = t["h_sb"]
    y_sb = t["y_sb"]
    gu_w, dw_w = t[L + "gu_w"], t[L + "dw_w"]
    ws = t["wstream"]
    s_w = L + "f_w"; s_mm = L + "f_mm"; s_sg = L + "f_sg"; s_h = L + "f_h"
    s_ev = L + "f_ev"; s_y = L + "f_y"; s_mmsub = L + "f_mmsub"
    s_set = L + "f_set"
    for nm in (s_w, s_mm, s_sg, s_h, s_ev, s_y, s_mmsub, s_set):
        kb.sem(nm)
    ccGU, ccDW = kb.c(L + "ccGU"), kb.c(L + "ccDW")

    for tg in range(2):
        tcol = tg * 512
        order = []
        for ic in range(44):
            order += [("g", ic), ("u", ic)]
        omm_all = kb.c(L + "o_mm")
        for idx, (gu, ic) in enumerate(order):
            oc = ic if gu == "g" else ic + 44
            rb, cl = oc // 11, (oc % 11) * 128
            slot = idx % 3
            pid = idx % 4
            sws = L + "f_w%d" % slot
            mmsub_now = kb.c(s_mmsub)
            def _ld(sync, oc=oc, rb=rb, cl=cl, slot=slot, idx=idx, tg=tg, sws=sws,
                    mmsub_now=mmsub_now):
                if idx == 0 and tg == 0:
                    kb.wait(sync, L + "ccGU", ccGU)
                    kb.wait(sync, L + "o_mm", omm_all)
                if idx < 3 and tg == 1:
                    kb.wait(sync, s_mmsub, mmsub_now)
                if idx >= 3:
                    kb.wait(sync, s_mm, max(0, kb.c(s_mm) - 2))
                for kq in range(0, NKC, 4):
                    kb.dma(sync, ws[:, slot * 2048 + kq * 128:slot * 2048 + (kq + 4) * 128],
                           gu_w[rb * 2048 + kq * 128:rb * 2048 + (kq + 4) * 128, cl:cl + 128]
                           .rearrange("(kc p) o -> p kc o", p=128), sws)
            block.sync(_ld)
            ldc = kb.c(sws)
            ln1ap = kb.c(L + "ln1_ap")
            def _mm(tensor, pid=pid, slot=slot, ldc=ldc, idx=idx, tg=tg, ln1ap=ln1ap,
                    sws=sws):
                kb.wait(tensor, sws, ldc)
                if idx == 0 and tg == 0:
                    kb.wait(tensor, L + "ln1_ap", ln1ap)
                if idx == 0 and tg == 1:
                    kb.wait(tensor, L + "ln2tg0_ap", kb.c(L + "ln2tg0_ap"))
                if idx >= 4 or tg > 0:
                    kb.wait(tensor, s_ev, max(0, kb.c(s_ev) - 3))
                pass
                last = None
                for kc in range(NKC):
                    last = tensor.matmul(P[pid][:, 0:512],
                                         ws[:, slot * 2048 + kc * 128:slot * 2048 + (kc + 1) * 128],
                                         x_work[:, kc * NT + tcol:kc * NT + tcol + 512],
                                         start=(kc == 0), stop=(kc == NKC - 1))
                kb.inc(last, s_mm, 1)
            block.tensor(_mm)
            mmc = kb.c(s_mm)
            if gu == "g":
                def _sg(scalar, pid=pid, mmc=mmc, ic=ic):
                    kb.wait(scalar, s_mm, mmc)
                    if ic > 0:
                        kb.wait(scalar, s_h, kb.c(s_h))
                    i = scalar.activation(out=t["sg_sb"][:, 0:512], in_=P[pid][:, 0:512],
                                          func=AF.Silu)
                    kb.inc(i, s_sg, 1)
                    kb.inc(scalar.nop(), s_ev, 1)
                block.scalar(_sg)
            else:
                sgc = kb.c(s_sg)
                def _h(vector, pid=pid, mmc=mmc, ic=ic, sgc=sgc):
                    kb.wait(vector, s_mm, mmc)
                    kb.wait(vector, s_sg, sgc)
                    i = vector.tensor_tensor(out=h_sb[:, ic * 512:(ic + 1) * 512],
                                             in0=t["sg_sb"][:, 0:512], in1=P[pid][:, 0:512],
                                             op=ALU.mult)
                    kb.inc(i, s_h, 1)
                    kb.inc(vector.engine_nop(), s_ev, 1)
                block.vector(_h)
        hc_all = kb.c(s_h)
        if cfg.get("stage") == "ffnh" and tg == 0:
            return
        if cfg.get("stage") == "ffnh1" and tg == 1:
            return
        IC_SUB = [(0, 16), (16, 32), (32, 44)]
        sub_i = 0
        for hc in range(NKC):
            pid = hc % 4
            ldcs = []
            for (i0, i1) in IC_SUB:
                slot = sub_i % 3
                sws = L + "f_w%d" % slot
                gu_mm_now = kb.c(s_mm)
                first3 = sub_i % 48 < 3
                def _ld(sync, hc=hc, tg=tg, i0=i0, i1=i1, slot=slot, sub_i=sub_i,
                        sws=sws, gu_mm_now=gu_mm_now, first3=first3):
                    if hc == 0 and tg == 0 and i0 == 0:
                        kb.wait(sync, L + "ccDW", ccDW)
                    if first3:
                        kb.wait(sync, s_mm, gu_mm_now)
                    else:
                        kb.wait(sync, s_mmsub, max(0, kb.c(s_mmsub) - 2))
                    for q0 in range(i0, i1, 4):
                        q1 = min(i1, q0 + 4)
                        kb.dma(sync, ws[:, slot * 2048 + (q0 - i0) * 128:slot * 2048 + (q1 - i0) * 128],
                               dw_w[(hc // 2) * 5632 + q0 * 128:(hc // 2) * 5632 + q1 * 128,
                                    (hc % 2) * 128:(hc % 2) * 128 + 128]
                               .rearrange("(ic p) o -> p ic o", p=128), sws)
                block.sync(_ld)
                ldc_now = kb.c(sws)
                def _settle(gpsimd, sws=sws, ldc_now=ldc_now):
                    kb.wait(gpsimd, sws, ldc_now)
                    kb.inc(gpsimd.nop(), s_set, 1)
                block.gpsimd(_settle)
                ldcs.append((ldc_now, slot, i0, i1, kb.c(s_set)))
                sub_i += 1
            def _mm(tensor, hc=hc, pid=pid, ldcs=ldcs, tg=tg):
                if hc == 0:
                    kb.wait(tensor, s_h, hc_all)
                if hc >= 4 or tg > 0:
                    kb.wait(tensor, s_y, max(0, kb.c(s_y) - 3))
                for (ldc, slot, i0, i1, setc) in ldcs:
                    kb.wait(tensor, L + "f_w%d" % slot, ldc)
                    kb.wait(tensor, s_set, setc)
                    last = None
                    for ic in range(i0, i1):
                        last = tensor.matmul(P[pid][:, 512:1024],
                                             ws[:, slot * 2048 + (ic - i0) * 128:slot * 2048 + (ic - i0 + 1) * 128],
                                             h_sb[:, ic * 512:(ic + 1) * 512],
                                             start=(ic == 0), stop=(ic == 43))
                    kb.inc(last, s_mmsub, 1)
            block.tensor(_mm)
            mmc = kb.c(s_mmsub)
            def _ev(vector, hc=hc, pid=pid, mmc=mmc):
                kb.wait(vector, s_mmsub, mmc)
                i = vector.tensor_copy(out=y_sb[:, hc * 512:(hc + 1) * 512],
                                       in_=P[pid][:, 512:1024])
                kb.inc(i, s_y, 1)
            block.vector(_ev)
        if cfg.get("stage") == "ffny" and tg == 0:
            return
        if cfg.get("stage") == "ffny1" and tg == 1:
            return
        if cfg.get("stage") == "ffnh1" and tg == 1:
            pass
        emit_rms_resid(kb, t, cfg, tg)
        if cfg.get("stage") == "rms0" and tg == 0:
            return
        emit_ln(kb, t, L + "ln2tg%d" % tg, src=x_resid, out_bf=x_work, cols=512,
                col0=tcol, gate=(L + "rms%d_ap" % tg, kb.c(L + "rms%d_ap" % tg)),
                store=cfg.get("store"))


def emit_rms_resid(kb, t, cfg, tg):
    nc, block = kb.nc, kb.block
    L = cfg["tag"]
    P = t["P"]
    tcol = tg * 512
    y_sb = t["y_sb"]
    rows = t["rows"]
    tag = L + "rms%d" % tg
    s_sq = tag + "_sq"; s_mm = tag + "_mm"; s_row = tag + "_row"
    s_bc = tag + "_bc"; s_ap = tag + "_ap"
    for nm in (s_sq, s_mm, s_row, s_bc, s_ap):
        kb.sem(nm)
    yc = kb.c(L + "f_y")
    sb = t["statbuf"]
    for kc in range(NKC):
        def _sq(scalar, kc=kc):
            if kc == 0:
                kb.wait(scalar, L + "f_y", yc)
            if kc >= 1:
                kb.wait(scalar, s_mm, kb.c(s_mm))
            i = scalar.activation(out=sb[:, 0:512],
                                  in_=y_sb[:, kc * 512:(kc + 1) * 512], func=AF.Square)
            kb.inc(i, s_sq, 1)
        block.scalar(_sq)
        sqc = kb.c(s_sq)
        def _mm(tensor, kc=kc, sqc=sqc):
            if kc == 0:
                kb.wait(tensor, L + "f_y", yc)
            kb.wait(tensor, s_sq, sqc)
            i = tensor.matmul(P[0][0:1, 0:512], t["ones_k"][:, 0:1], sb[:, 0:512],
                              start=(kc == 0), stop=(kc == NKC - 1))
            kb.inc(i, s_mm, 1)
        block.tensor(_mm)
    mmc = kb.c(s_mm)
    def _row(vector, mmc=mmc):
        kb.wait(vector, s_mm, mmc)
        i = vector.tensor_scalar(out=rows[0:1, 0:512], in0=P[0][0:1, 0:512],
                                 scalar1=1.0 / 2048.0, scalar2=None, op0=ALU.mult)
        kb.inc(i, s_row, 1)
    block.vector(_row)
    rc = kb.c(s_row)
    def _rstd(scalar, rc=rc):
        kb.wait(scalar, s_row, rc)
        scalar.activation(out=rows[0:1, 2048:2560], in_=rows[0:1, 0:512],
                          func=AF.Ln, bias=f32(RMS_EPS))
        i = scalar.activation(out=rows[0:1, 0:512], in_=rows[0:1, 2048:2560],
                              func=AF.Exp, scale=f32(-0.5))
        kb.inc(i, s_row, 1)
    block.scalar(_rstd)
    rc3 = kb.c(s_row)
    def _bc(tensor, rc3=rc3):
        kb.wait(tensor, s_row, rc3)
        i = tensor.matmul(P[0][:, 0:512], t["ones_1t_f"][0:1, :], rows[0:1, 0:512],
                          start=True, stop=True)
        kb.inc(i, s_bc, 1)
    block.tensor(_bc)
    bcc = kb.c(s_bc)
    def _bcev(vector, bcc=bcc):
        kb.wait(vector, s_bc, bcc)
        i = vector.tensor_copy(out=t["bc_sb"][:, 0:512], in_=P[0][:, 0:512])
        kb.inc(i, s_bc, 1)
    block.vector(_bcev)
    bcc2 = kb.c(s_bc)
    for kc in range(NKC):
        def _ap(vector, kc=kc):
            kb.wait(vector, s_bc, bcc2)
            vector.tensor_tensor(out=t["tmpf"][:, 0:512],
                                 in0=y_sb[:, kc * 512:(kc + 1) * 512],
                                 in1=t["bc_sb"][:, 0:512], op=ALU.mult)
            i = vector.tensor_tensor(
                out=t["x_resid"][:, kc * NT + tcol:kc * NT + tcol + 512],
                in0=t["x_resid"][:, kc * NT + tcol:kc * NT + tcol + 512],
                in1=t["tmpf"][:, 0:512], op=ALU.add)
            kb.inc(i, s_ap, 1)
        block.vector(_ap)



def emit_dequant(kb, t, cfg):
    nc, block = kb.nc, kb.block
    L = cfg["tag"]
    dq = t["dq"]
    s_src = L + "dq_src"; s_u = L + "dq_u"; s_g = L + "dq_g"; s_err = L + "dq_err"
    s_w = L + "dq_w"; s_out = L + "dq_out"; s_done = L + "dq_done"
    for nm in (s_src, s_u, s_g, s_err, s_w, s_out, s_done):
        kb.sem(nm)

    it_box = [0]
    def one_matrix(src_ap, sc_ap, out_ap, n_ch, ncols, HW, shift_plan,
                   row_plan):
        nh = ncols // HW
        for q in range(n_ch):
            for hh in range(nh):
                it = it_box[0]
                slot = it % 2
                c0, c1 = hh * HW, (hh + 1) * HW
                uc_need = kb.c(s_u)
                def _ld(sync, q=q, hh=hh, slot=slot, it=it, uc_need=uc_need,
                        c0=c0, c1=c1):
                    if it == 0 and cfg.get("dq_gate") is not None:
                        kb.wait(sync, cfg["dq_gate"][0], cfg["dq_gate"][1])
                    if it >= 2:
                        kb.wait(sync, s_u, uc_need - 1)
                        kb.wait(sync, s_w, max(0, kb.c(s_w) - 1))
                    for (pr0, prn, sr0) in row_plan(q):
                        kb.dma(sync, dq["psrc"][pr0:pr0 + prn,
                                                slot * HW:slot * HW + HW],
                               src_ap[sr0:sr0 + prn, c0:c1], s_src + str(slot))
                    kb.dma(sync, dq["sc"][:, slot * HW:slot * HW + HW],
                           sc_ap[q * 128:(q + 1) * 128, c0:c1], s_src + str(slot))
                block.sync(_ld)
                ldc = kb.c(s_src + str(slot))
                def _int(vector, q=q, slot=slot, it=it, ldc=ldc):
                    kb.wait(vector, s_src + str(slot), ldc)
                    kb.wait(vector, s_err, kb.c(s_err))
                    if it >= 2:
                        kb.wait(vector, s_out + str(slot), kb.c(s_out + str(slot)))
                    ps = dq["psrc"][:, slot * HW:slot * HW + HW]
                    B, C = dq["B"][:, 0:HW], dq["C"][:, 0:HW]
                    ja, jb, mcol = shift_plan(q)
                    if jb is None:
                        vector.tensor_scalar(out=ps, in0=ps, scalar1=4 * ja,
                                             scalar2=15,
                                             op0=ALU.logical_shift_right,
                                             op1=ALU.bitwise_and)
                    else:
                        vector.tensor_scalar(out=B, in0=ps, scalar1=4 * ja, scalar2=15,
                                             op0=ALU.logical_shift_right,
                                             op1=ALU.bitwise_and)
                        vector.tensor_scalar(out=C, in0=ps, scalar1=4 * jb, scalar2=15,
                                             op0=ALU.logical_shift_right,
                                             op1=ALU.bitwise_and)
                        vector.tensor_scalar(out=B, in0=B, scalar1=mcol[0],
                                             scalar2=None, op0=ALU.bitwise_and)
                        vector.tensor_scalar(out=C, in0=C, scalar1=mcol[1],
                                             scalar2=None, op0=ALU.bitwise_and)
                        vector.tensor_tensor(out=ps, in0=B, in1=C,
                                             op=ALU.bitwise_or)
                    vector.tensor_scalar(out=B, in0=ps, scalar1=22,
                                         scalar2=0x01C00000,
                                         op0=ALU.logical_shift_left,
                                         op1=ALU.bitwise_and)
                    i = vector.tensor_scalar(out=C, in0=ps, scalar1=8, scalar2=28,
                                             op0=ALU.bitwise_and,
                                             op1=ALU.logical_shift_left)
                    kb.inc(i, s_u, 1)
                    i = vector.tensor_scalar(out=B, in0=B,
                                             scalar1=0x3F000000, scalar2=None,
                                             op0=ALU.add)
                    kb.inc(i, s_g, 1)
                block.vector(_int)
                gc = kb.c(s_g)
                def _err(scalar, gc=gc):
                    kb.wait(scalar, s_g, gc)
                    i = scalar.activation(out=dq["Derr"][:, 0:HW], in_=dq["Bf"][:, 0:HW],
                                          func=AF.Relu, scale=f32(-1.0), bias=f32(1.0))
                    kb.inc(i, s_err, 1)
                block.scalar(_err)
                ec = kb.c(s_err)
                def _fin(vector, slot=slot, ec=ec):
                    kb.wait(vector, s_err, ec)
                    vector.tensor_tensor(out=dq["Bf"][:, 0:HW], in0=dq["Bf"][:, 0:HW],
                                         in1=dq["Derr"][:, 0:HW], op=ALU.subtract)
                    vector.tensor_tensor(out=dq["B"][:, 0:HW], in0=dq["B"][:, 0:HW],
                                         in1=dq["C"][:, 0:HW], op=ALU.bitwise_or)
                    i = vector.tensor_tensor(out=dq["wt"][:, slot * HW:slot * HW + HW],
                                             in0=dq["Bf"][:, 0:HW],
                                             in1=dq["sc"][:, slot * HW:slot * HW + HW],
                                             op=ALU.mult)
                    kb.inc(i, s_w, 1)
                block.vector(_fin)
                wc = kb.c(s_w)
                def _st(gpsimd, q=q, slot=slot, c0=c0, c1=c1, wc=wc):
                    kb.wait(gpsimd, s_w, wc)
                    kb.dma(gpsimd, out_ap[q * 128:(q + 1) * 128, c0:c1],
                           dq["wt"][:, slot * HW:slot * HW + HW], s_out + str(slot))
                block.gpsimd(_st)
                it_box[0] = it + 1

    def gu_rows(q):
        return [(0, 128, (q % 2) * 128)]
    one_matrix(t[L + "gu_pk"], t[L + "gu_sc"], t[L + "gu_in"], 16, GU_SL, 704,
               lambda q: (q // 2, None, None), gu_rows)
    def dw_rows(q):
        base = (128 * q) % 704
        if base + 128 <= 704:
            return [(0, 128, base)]
        n1 = 704 - base
        return [(0, n1, base), (n1, 128 - n1, 0)]
    def dw_shift_plan(q):
        ja = (128 * q) // 704
        jb = (128 * q + 127) // 704
        if ja == jb:
            return (ja, None, None)
        return (ja, jb, (t["jmaskA"][:, q:q + 1], t["jmaskB"][:, q:q + 1]))
    one_matrix(t[L + "dw_pk"], t[L + "dw_sc"], t[L + "dw_in"], 44, DW_SL, 256,
               dw_shift_plan, dw_rows)

    outc0, outc1 = kb.c(s_out + "0"), kb.c(s_out + "1")
    def _cc(gpsimd, outc0=outc0, outc1=outc1):
        kb.wait(gpsimd, s_out + "0", outc0)
        kb.wait(gpsimd, s_out + "1", outc1)
        i = gpsimd.collective_compute(
            "AllGather", ALU.bypass, replica_groups=[list(range(8))],
            ins=[t[L + "gu_in"].opt()], outs=[t[L + "gu_w"].opt()])
        kb.inc(i, L + "ccGU", 1)
        i = gpsimd.collective_compute(
            "AllGather", ALU.bypass, replica_groups=[list(range(8))],
            ins=[t[L + "dw_in"].opt()], outs=[t[L + "dw_w"].opt()])
        kb.inc(i, L + "ccDW", 1)
        kb.inc(gpsimd.nop(), s_done, 1)
    block.gpsimd(_cc)


def build_launch(layers, final, entry_has_xw, stage=None, race_detect=False):
    es = contextlib.ExitStack()
    nc = bass.Bass("TRN2", target_bir_lowering=False,
                   detect_race_conditions=race_detect)
    t = {}
    names = []

    def inp(name, shape, dt):
        names.append(name)
        return nc.dram_tensor(name, shape, dt, kind="ExternalInput").ap()

    xin = inp("xin", [D, NT], F32)
    t["masks_in"] = None
    if entry_has_xw:
        xwin = inp("xwin", [D, NT], BF16)
    masks_in = inp("masks", [128, 32 * 128], BF16)
    t_masks_in = masks_in
    if stage == "dq":
        dbg_gu = nc.dram_tensor("dbg_gu", [8 * D, GU_SL], BF16, kind="ExternalOutput").ap()
        dbg_dw = nc.dram_tensor("dbg_dw", [8 * IFFN, DW_SL], BF16, kind="ExternalOutput").ap()
    elif stage == "ffnh":
        dbg_h = nc.dram_tensor("dbg_h", [IFFN, 512], BF16, kind="ExternalOutput").ap()
    elif stage in ("ffny", "ffny1"):
        dbg_y = nc.dram_tensor("dbg_y", [D, 512], BF16, kind="ExternalOutput").ap()
        dbg_dw = nc.dram_tensor("dbg_dw", [8 * IFFN, DW_SL], BF16, kind="ExternalOutput").ap()
        dbg_h = nc.dram_tensor("dbg_h", [IFFN, 512], BF16, kind="ExternalOutput").ap()
        dbg_ws = nc.dram_tensor("dbg_ws", [128, 6144], BF16, kind="ExternalOutput").ap()
    elif stage == "rms0":
        dbg_x2 = nc.dram_tensor("dbg_x2", [D, 512], F32, kind="ExternalOutput").ap()
        dbg_r = nc.dram_tensor("dbg_r", [128, 512], F32, kind="ExternalOutput").ap()
        dbg_y2 = nc.dram_tensor("dbg_y2", [D, 512], BF16, kind="ExternalOutput").ap()
    elif stage == "ffnh1":
        dbg_h = nc.dram_tensor("dbg_h", [IFFN, 512], BF16, kind="ExternalOutput").ap()
    elif stage in ("q", "qk", "q_nodq"):
        dbg_q = nc.dram_tensor("dbg_q", [D, NT], BF16, kind="ExternalOutput").ap()
        kvc0 = layers[0]["kv_chunks"]
        dbg_K = nc.dram_tensor("dbg_K", [kvc0 * 128, NT], BF16, kind="ExternalOutput").ap()
    elif stage in ("qkv", "attn", "attn_nodq"):
        kvc0 = layers[0]["kv_chunks"]
        dbg_q = nc.dram_tensor("dbg_q", [D, NT], BF16, kind="ExternalOutput").ap()
        dbg_K = nc.dram_tensor("dbg_K", [4 * kvc0 * 128, NT], BF16, kind="ExternalOutput").ap()
        dbg_V = nc.dram_tensor("dbg_V", [4 * NT, kvc0 * 128], BF16, kind="ExternalOutput").ap()
    elif final:
        out_fm = nc.dram_tensor("out_fm", [D, NT], F32, kind="ExternalOutput").ap()
    else:
        xres_out = nc.dram_tensor("xres_out", [D, NT], F32, kind="ExternalOutput").ap()
        xw_out = nc.dram_tensor("xw_out", [D, NT], BF16, kind="ExternalOutput").ap()

    for cfg in layers:
        L = cfg["tag"]
        kvc = cfg["kv_chunks"]
        cfg["wq"] = inp(L + "wq", [D, D], F32)
        cfg["wk"] = inp(L + "wk", [D, kvc * 128], F32)
        cfg["wv"] = inp(L + "wv", [D, kvc * 128], F32)
        cfg["wo"] = inp(L + "wo", [D, D], F32)
        t["bq_" + L] = inp(L + "bq", [128, NH], F32)
        t["bk_" + L] = inp(L + "bk", [128, kvc], F32)
        t["bv_" + L] = inp(L + "bv", [1, kvc * 128], F32)
        if cfg["bo"] is not None:
            t["bo_" + L] = inp(L + "bo", [128, NKC], F32)
        if cfg["pe"]:
            t["pe_" + L] = inp(L + "pe", [128, NT], BF16)
        t[L + "gu_pk"] = inp(L + "gu_pk", [256, GU_SL], I32)
        t[L + "gu_sc"] = inp(L + "gu_sc", [D, GU_SL], F16)
        t[L + "dw_pk"] = inp(L + "dw_pk", [704, DW_SL], I32)
        t[L + "dw_sc"] = inp(L + "dw_sc", [IFFN, DW_SL], F16)
        t[L + "K_in"] = nc.dram_tensor(L + "K_in", [kvc * 128, NT], BF16).ap()
        t[L + "K_g"] = nc.dram_tensor(L + "K_g", [4 * kvc * 128, NT], BF16).ap()
        t[L + "V_in"] = nc.dram_tensor(L + "V_in", [NT, kvc * 128], BF16).ap()
        t[L + "V_g"] = nc.dram_tensor(L + "V_g", [4 * NT, kvc * 128], BF16).ap()
        t[L + "gu_in"] = nc.dram_tensor(L + "gu_in", [D, GU_SL], BF16).ap()
        t[L + "gu_w"] = nc.dram_tensor(L + "gu_w", [8 * D, GU_SL], BF16).ap()
        t[L + "dw_in"] = nc.dram_tensor(L + "dw_in", [IFFN, DW_SL], BF16).ap()
        t[L + "dw_w"] = nc.dram_tensor(L + "dw_w", [8 * IFFN, DW_SL], BF16).ap()
    t["jmaskA_in"] = inp("jmaskA", [128, 44], I32)
    t["jmaskB_in"] = inp("jmaskB", [128, 44], I32)
    t["masks_dram"] = masks_in

    with es:
        sb = lambda name, shape, dt: es.enter_context(nc.sbuf_tensor(name, shape, dt))
        ps = lambda name: es.enter_context(nc.psum_tensor(name, [128, 1024], F32))
        t["x_resid"] = sb("x_resid", [128, NKC * NT], F32).ap()
        XW = sb("x_work", [128, NKC * NT], BF16)
        t["x_work"] = XW.ap()
        t["masks"] = XW.ap()[:, 0:4096]
        t["probs"] = XW.ap()[:, 4096:6144]
        t["kring"] = XW.ap()[:, 6144:7168]
        t["vring"] = XW.ap()[:, 7168:8192]
        RING = 8
        t["RING"] = RING
        SCR = sb("SCR", [128, 2 * NKC * NT], BF16)
        t["qf"] = SCR.ap()[:, 0:NKC * NT]
        t["af"] = SCR.ap()[:, NKC * NT:2 * NKC * NT]
        t["h_sb"] = SCR.ap()[:, 0:44 * 512]
        t["y_sb"] = SCR.ap()[:, 44 * 512:44 * 512 + NKC * 512]
        afi = SCR.ap().bitcast(I32)
        aff = SCR.ap().bitcast(F32)
        afh = SCR.ap().bitcast(F16)
        base = 8192
        t["dq"] = {
            "psrc": afi[:, base + 0: base + 1408],
            "B": afi[:, base + 1408: base + 2112],
            "Bf": aff[:, base + 1408: base + 2112],
            "C": afi[:, base + 2112: base + 2816],
            "Derr": aff[:, base + 2816: base + 3520],
            "wt": SCR.ap()[:, 2 * (base + 3520): 2 * (base + 3520) + 1408],
            "sc": afh[:, 2 * (base + 4224): 2 * (base + 4224) + 1408],
        }
        WB = sb("wbig", [128, 6144], BF16)
        t["wcache"] = WB.ap()[:, 0:4096]
        t["wslab"] = WB.ap().bitcast(F32)[:, 2048:3072]
        t["wstream"] = WB.ap()[:, 0:6144]
        t["kstage"] = sb("kstage", [128, NT], BF16).ap()
        t["vstage"] = sb("vstage", [128, 512], BF16).ap()
        ROWS = sb("rows", [1, 3072], F32)
        t["rows"] = ROWS.ap()
        t["bvrow"] = ROWS.ap().bitcast(BF16)[0:1, 0:2048]
        t["bc_sb"] = sb("bc_sb", [128, 2048], F32).ap()
        t["statbuf"] = sb("statbuf", [128, 2048], BF16).ap()
        t["tmpf"] = sb("tmpf", [128, 1024], F32).ap()
        t["tmpf2"] = t["tmpf"][:, 512:1024]
        t["ones_k"] = sb("ones_k", [128, 1], BF16).ap()
        t["ones_1t"] = sb("ones_1t", [1, 128], BF16).ap()
        t["ones_1t_f"] = sb("ones_1t_f", [1, 128], F32).ap()
        t["sg_sb"] = sb("sg_sb", [128, 512], BF16).ap()
        t["biases"] = sb("biases", [128, 64], F32).ap()
        t["pe_own"] = sb("pe_own", [128, NT], BF16).ap()
        t["jmaskA"] = sb("jmaskA_sb", [128, 44], I32).ap()
        t["jmaskB"] = sb("jmaskB_sb", [128, 44], I32).ap()
        t["P"] = [ps("P%d" % i).ap() for i in range(4)]

        block = es.enter_context(nc.Block())
        kb = KB(nc, block, es)

        def _init(sync):
            for kc in range(NKC):
                kb.dma(sync, t["x_resid"][:, kc * NT:(kc + 1) * NT],
                       xin[kc * 128:(kc + 1) * 128, :], "ld_x")
                if entry_has_xw:
                    kb.dma(sync, t["x_work"][:, kc * NT:(kc + 1) * NT],
                           xwin[kc * 128:(kc + 1) * 128, :], "ld_x")
            kb.dma(sync, t["jmaskA"], t["jmaskA_in"][:, :], "ld_x")
            kb.dma(sync, t["jmaskB"], t["jmaskB_in"][:, :], "ld_x")
        block.sync(_init)
        ldx = kb.c("ld_x")
        def _mset(vector):
            kb.wait(vector, "ld_x", ldx)
            vector.memset(t["ones_k"], 1.0)
            vector.memset(t["ones_1t"], 1.0)
            i = vector.memset(t["ones_1t_f"], 1.0)
            kb.inc(i, "entry", 1)
        block.vector(_mset)

        for li, cfg in enumerate(layers):
            L = cfg["tag"]
            t["bq"], t["bk"] = t["bq_" + L], t["bk_" + L]
            t["bo"] = t.get("bo_" + L)
            t["K_in"], t["K_g"] = t[L + "K_in"], t[L + "K_g"]
            t["V_in"], t["V_g"] = t[L + "V_in"], t[L + "V_g"]
            bqs = t["biases"][:, 0:NH]
            bks = t["biases"][:, NH:NH + cfg["kv_chunks"]]
            bos = t["biases"][:, 32:48]
            prev_gate = None
            if li > 0:
                pL = layers[li - 1]["tag"]
                prev_gate = (pL + "ln2tg1_ap", kb.c(pL + "ln2tg1_ap"))
                cfg["dq_gate"] = (pL + "rms1_ap", kb.c(pL + "rms1_ap"))
                cfg["entry_gate"] = prev_gate
                cfg["q_war"] = (pL + "f_mmsub", kb.c(pL + "f_mmsub"))
            def _lb2(sync, cfg=cfg, L=L, bqs=bqs, bks=bks, bos=bos,
                     prev_gate=prev_gate):
                if prev_gate is not None:
                    kb.wait(sync, prev_gate[0], prev_gate[1])
                kb.dma(sync, bqs, t["bq_" + L][:, :], "ld_b")
                kb.dma(sync, bks, t["bk_" + L][:, :], "ld_b")
                if cfg["bo"] is not None:
                    kb.dma(sync, bos, t["bo_" + L][:, :], "ld_b")
                if cfg["pe"]:
                    kb.dma(sync, t["pe_own"], t["pe_" + L][:, :], "ld_b")
                kb.dma(sync, t["rows"][0:1, 1024:1024 + cfg["kv_chunks"] * 128],
                       t["bv_" + L][0:1, :], "ld_b")
            block.sync(_lb2)
            ldb = kb.c("ld_b")
            def _cvb(vector, cfg=cfg, ldb=ldb):
                kb.wait(vector, "ld_b", ldb)
                i = vector.tensor_copy(out=t["bvrow"][0:1, 0:cfg["kv_chunks"] * 128],
                                       in_=t["rows"][0:1, 1024:1024 + cfg["kv_chunks"] * 128])
                kb.inc(i, "entry", 1)
            block.vector(_cvb)
            t["bq"], t["bk"] = bqs, bks
            t["bo"] = bos if cfg["bo"] is not None else None
            if stage not in ("q_nodq", "attn_nodq"):
                emit_dequant(kb, t, cfg)
            if li == 0:
                kb.sem("xw_rdy")
                if not entry_has_xw:
                    def _cast(vector):
                        kb.wait(vector, "ld_x", ldx)
                        for kc in range(NKC):
                            i = vector.tensor_copy(out=t["x_work"][:, kc * NT:(kc + 1) * NT],
                                                   in_=t["x_resid"][:, kc * NT:(kc + 1) * NT])
                        kb.inc(i, "xw_rdy", 1)
                    block.vector(_cast)
                else:
                    def _xwv(vector):
                        kb.wait(vector, "ld_x", ldx)
                        kb.inc(vector.engine_nop(), "xw_rdy", 1)
                    block.vector(_xwv)
                cfg["entry_gate"] = ("xw_rdy", 1)
            if final and li == len(layers) - 1:
                cfg["store"] = out_fm
            cfg["stage"] = stage
            emit_layer(kb, t, cfg)

        last = layers[-1]["tag"]
        if stage == "rms0":
            L = layers[0]["tag"]
            apn = kb.c(L + "rms0_ap")
            def _dumpr(gpsimd):
                kb.wait(gpsimd, L + "rms0_ap", apn)
                for kc in range(NKC):
                    kb.dma(gpsimd, dbg_x2[kc * 128:(kc + 1) * 128, :],
                           t["x_resid"][:, kc * NT:kc * NT + 512], "st_out")
                    kb.dma(gpsimd, dbg_y2[kc * 128:(kc + 1) * 128, :],
                           t["y_sb"][:, kc * 512:(kc + 1) * 512], "st_out")
                kb.dma(gpsimd, dbg_r, t["bc_sb"][:, 0:512], "st_out")
                kb.wait(gpsimd, "st_out")
            block.gpsimd(_dumpr)
        elif stage in ("ffny", "ffny1"):
            L = layers[0]["tag"]
            yn = kb.c(L + "f_y")
            def _dumpy(gpsimd):
                kb.wait(gpsimd, L + "f_y", yn)
                for kc in range(NKC):
                    kb.dma(gpsimd, dbg_y[kc * 128:(kc + 1) * 128, :],
                           t["y_sb"][:, kc * 512:(kc + 1) * 512], "st_out")
                kb.dma(gpsimd, dbg_dw.opt(), t[L + "dw_w"].opt(), "st_out")
                for ic in range(44):
                    kb.dma(gpsimd, dbg_h[ic * 128:(ic + 1) * 128, :],
                           t["h_sb"][:, ic * 512:(ic + 1) * 512], "st_out")
                kb.dma(gpsimd, dbg_ws, t["wstream"][:, :], "st_out")
                kb.wait(gpsimd, "st_out")
            block.gpsimd(_dumpy)
        elif stage in ("ffnh", "ffnh1"):
            L = layers[0]["tag"]
            hn = kb.c(L + "f_h")
            def _dumph(gpsimd):
                kb.wait(gpsimd, L + "f_h", hn)
                for ic in range(44):
                    kb.dma(gpsimd, dbg_h[ic * 128:(ic + 1) * 128, :],
                           t["h_sb"][:, ic * 512:(ic + 1) * 512], "st_out")
                kb.wait(gpsimd, "st_out")
            block.gpsimd(_dumph)
        elif stage in ("q", "qk", "q_nodq"):
            L = layers[0]["tag"]
            qn = kb.c(L + "q_ready")
            kn = kb.c(L + "k_st")
            def _dumpq(gpsimd):
                kb.wait(gpsimd, L + "q_ready", qn)
                for kc in range(NKC):
                    kb.dma(gpsimd, dbg_q[kc * 128:(kc + 1) * 128, :],
                           t["qf"][:, kc * NT:(kc + 1) * NT], "st_out")
                if kn:
                    kb.wait(gpsimd, L + "k_st", kn)
                    kb.dma(gpsimd, dbg_K.opt(), t[L + "K_in"].opt(), "st_out")
                kb.wait(gpsimd, "st_out")
            block.gpsimd(_dumpq)
        elif stage == "dq":
            L = layers[0]["tag"]
            cgu, cdw = kb.c(L + "ccGU"), kb.c(L + "ccDW")
            def _dumpd(gpsimd):
                kb.wait(gpsimd, L + "ccGU", cgu)
                kb.wait(gpsimd, L + "ccDW", cdw)
                kb.dma(gpsimd, dbg_gu.opt(), t[L + "gu_w"].opt(), "st_out")
                kb.dma(gpsimd, dbg_dw.opt(), t[L + "dw_w"].opt(), "st_out")
                kb.wait(gpsimd, "st_out")
            block.gpsimd(_dumpd)
        elif stage in ("qkv", "attn", "attn_nodq"):
            L = layers[0]["tag"]
            src_t = t["qf"] if stage == "qkv" else t["af"]
            gsem = (L + "q_ready", kb.c(L + "q_ready")) if stage == "qkv" else \
                   (L + "a_afe", kb.c(L + "a_afe"))
            ck, cv2 = kb.c(L + "ccK"), kb.c(L + "ccV")
            def _dump(gpsimd):
                kb.wait(gpsimd, gsem[0], gsem[1])
                kb.wait(gpsimd, L + "ccK", ck)
                kb.wait(gpsimd, L + "ccV", cv2)
                for kc in range(NKC):
                    kb.dma(gpsimd, dbg_q[kc * 128:(kc + 1) * 128, :],
                           src_t[:, kc * NT:(kc + 1) * NT], "st_out")
                kb.dma(gpsimd, dbg_K.opt(), t["K_g"].opt(), "st_out")
                kb.dma(gpsimd, dbg_V.opt(), t["V_g"].opt(), "st_out")
                kb.wait(gpsimd, "st_out")
            block.gpsimd(_dump)
        elif stage == "wo":
            L = layers[0]["tag"]
            apl1 = kb.c(L + "ln1_ap")
            def _dumpw(gpsimd):
                kb.wait(gpsimd, L + "ln1_ap", apl1)
                for kc in range(NKC):
                    kb.dma(gpsimd, xres_out[kc * 128:(kc + 1) * 128, :],
                           t["x_resid"][:, kc * NT:(kc + 1) * NT], "st_out")
                    kb.dma(gpsimd, xw_out[kc * 128:(kc + 1) * 128, :],
                           t["x_work"][:, kc * NT:(kc + 1) * NT], "st_out")
                kb.wait(gpsimd, "st_out")
            block.gpsimd(_dumpw)
        elif final:
            c0 = kb.c(last + "ln2tg0_st")
            c1 = kb.c(last + "ln2tg1_st")
            def _out(gpsimd):
                kb.wait(gpsimd, last + "ln2tg0_st", c0)
                kb.wait(gpsimd, last + "ln2tg1_st", c1)
            block.gpsimd(_out)
        else:
            apc0 = kb.c(last + "ln2tg0_ap")
            apc = kb.c(last + "ln2tg1_ap")
            apl1 = kb.c(last + "ln1_ap")
            def _out(gpsimd):
                kb.wait(gpsimd, last + "ln1_ap", apl1)
                kb.wait(gpsimd, last + "ln2tg0_ap", apc0)
                kb.wait(gpsimd, last + "ln2tg1_ap", apc)
                for kc in range(NKC):
                    kb.dma(gpsimd, xres_out[kc * 128:(kc + 1) * 128, :],
                           t["x_resid"][:, kc * NT:(kc + 1) * NT], "st_out")
                    kb.dma(gpsimd, xw_out[kc * 128:(kc + 1) * 128, :],
                           t["x_work"][:, kc * NT:(kc + 1) * NT], "st_out")
                kb.wait(gpsimd, "st_out")
            block.gpsimd(_out)
    return nc, names


def _sinusoidal_pe(tmax, d):
    pos = np.arange(tmax, dtype=np.float32)[:, None]
    i = np.arange(0, d, 2, dtype=np.float32)
    inv = np.exp(-(np.log(10000.0) * i / d)).astype(np.float32)
    ang = pos * inv
    pe = np.zeros((tmax, d), np.float32)
    pe[:, 0::2] = np.sin(ang)
    pe[:, 1::2] = np.cos(ang)
    return pe


def _core_tokens(r):
    rr = r % 4
    tiles = [4 * l + rr for l in range(8)]
    return np.concatenate([np.arange(128 * ti, 128 * (ti + 1)) for ti in tiles]), tiles


def _masks_for_core(r):
    rr = r % 4
    m = np.zeros((128, 32 * 128), np.float32)
    for mm in range(8):
        for rho in range(4):
            blk = m[:, (mm * 4 + rho) * 128:(mm * 4 + rho + 1) * 128]
            if rho < rr:
                blk[:] = 1.0
            elif rho == rr:
                kk, qq = np.meshgrid(np.arange(128), np.arange(128), indexing="ij")
                blk[:] = (kk <= qq).astype(np.float32)
    return m.astype(BF)


def _ffn_host(layer_inputs, prefix):
    gu_p = np.asarray(layer_inputs[prefix + "gate_up_packed"])
    gu_s = np.asarray(layer_inputs[prefix + "gate_up_scales"])
    dw_p = np.asarray(layer_inputs[prefix + "down_packed"])
    dw_s = np.asarray(layer_inputs[prefix + "down_scales"])
    o_perm = np.concatenate([PI_I, IFFN + PI_I])
    guT = np.ascontiguousarray(gu_p[o_perm].T)
    g_of_h = (PI_H // 128).astype(np.int64)
    gu_sc = np.ascontiguousarray(gu_s[:, o_perm][g_of_h])
    dwT = np.ascontiguousarray(dw_p.T[:, PI_H])
    g_of_i = (PI_I // 128).astype(np.int64)
    dw_sc = np.ascontiguousarray(dw_s[:, PI_H][g_of_i])
    return guT, gu_sc, dwT, dw_sc


def kernel(**inputs):
    X = np.asarray(inputs["L0_attn_X"], np.float32)
    pe_t = _sinusoidal_pe(S, DH)

    cfg0 = dict(tag="A", kv_chunks=16, pe=False, bo=1)
    cfg1 = dict(tag="B", kv_chunks=4, pe=True, bo=None)

    com0 = {
        "Awq": np.ascontiguousarray(inputs["L0_attn_Wq"].T[PI_H], np.float32),
        "Awk": np.ascontiguousarray(inputs["L0_attn_Wk"].T[PI_H], np.float32),
        "Awv": np.ascontiguousarray(inputs["L0_attn_Wv"].T[PI_H], np.float32),
        "Awo": np.ascontiguousarray(np.asarray(inputs["L0_attn_Wo"]).T[:, PI_H], np.float32),
        "Abq": np.ascontiguousarray(np.asarray(inputs["L0_attn_bq"]).reshape(16, 128).T, np.float32),
        "Abk": np.ascontiguousarray(np.asarray(inputs["L0_attn_bk"]).reshape(16, 128).T, np.float32),
        "Abv": np.asarray(inputs["L0_attn_bv"], np.float32)[None, :].copy(),
        "Abo": np.ascontiguousarray(np.asarray(inputs["L0_attn_bo"])[PI_H].reshape(16, 128).T, np.float32),
    }
    jj = (128 * np.arange(44)[None, :] + np.arange(128)[:, None]) // 704
    ja_ = (128 * np.arange(44)) // 704
    com0["jmaskA"] = np.where(jj == ja_[None, :], -1, 0).astype(np.int32)
    com0["jmaskB"] = np.where(jj != ja_[None, :], -1, 0).astype(np.int32)
    kcols = np.concatenate([np.arange(256 * g, 256 * g + 128) for g in range(4)])
    vcols = np.concatenate([np.arange(256 * g + 128, 256 * (g + 1)) for g in range(4)])
    wkv = np.asarray(inputs["L1_attn_wkv"], np.float32)
    bkv = np.asarray(inputs["L1_attn_bkv"], np.float32)
    com1 = {
        "Bwq": np.ascontiguousarray(np.asarray(inputs["L1_attn_wq"], np.float32)[PI_H]),
        "Bwk": np.ascontiguousarray(wkv[PI_H][:, kcols]),
        "Bwv": np.ascontiguousarray(wkv[PI_H][:, vcols]),
        "Bwo": np.ascontiguousarray(np.asarray(inputs["L1_attn_wo"], np.float32)[:, PI_H]),
        "Bbq": np.ascontiguousarray(np.asarray(inputs["L1_attn_bq"], np.float32).reshape(16, 128).T),
        "Bbk": np.ascontiguousarray(bkv[kcols].reshape(4, 128).T),
        "Bbv": bkv[vcols][None, :].copy(),
        "jmaskA": com0["jmaskA"], "jmaskB": com0["jmaskB"],
    }
    guT0, gusc0, dwT0, dwsc0 = _ffn_host(inputs, "L0_ffn_")
    guT1, gusc1, dwT1, dwsc1 = _ffn_host(inputs, "L1_ffn_")

    per_core0, per_core1, toks = [], [], []
    for r in range(8):
        b = r // 4
        tok, _ = _core_tokens(r)
        toks.append(tok)
        masks = _masks_for_core(r)
        sl = slice(GU_SL * r, GU_SL * (r + 1))
        sld = slice(DW_SL * r, DW_SL * (r + 1))
        pc0 = {
            "xin": np.ascontiguousarray(X[b].T[PI_H][:, tok], np.float32),
            "masks": masks,
            "Agu_pk": np.ascontiguousarray(guT0[:, sl]),
            "Agu_sc": np.ascontiguousarray(gusc0[:, sl]),
            "Adw_pk": np.ascontiguousarray(dwT0[:, sld]),
            "Adw_sc": np.ascontiguousarray(dwsc0[:, sld]),
        }
        pc1 = {
            "masks": masks,
            "Bpe": np.ascontiguousarray(pe_t[tok % S].T).astype(BF),
            "Bgu_pk": np.ascontiguousarray(guT1[:, sl]),
            "Bgu_sc": np.ascontiguousarray(gusc1[:, sl]),
            "Bdw_pk": np.ascontiguousarray(dwT1[:, sld]),
            "Bdw_sc": np.ascontiguousarray(dwsc1[:, sld]),
        }
        per_core0.append(pc0)
        per_core1.append(pc1)

    nc0, _ = build_launch([cfg0, cfg1], final=True, entry_has_xw=False)
    in_maps = [{**com0, **com1, **per_core0[r], **per_core1[r]}
               for r in range(8)]
    res1 = run_bass_kernel_spmd(nc0, in_maps, core_ids=list(range(8)))

    out = np.empty((2, S, D), np.float32)
    inv = np.empty(D, np.int64)
    inv[PI_H] = np.arange(D)
    for r in range(8):
        b = r // 4
        o = res1.results[r]["out_fm"]
        out[b, toks[r], :] = o[inv].T
    return out
